# version 06
# baseline (speedup 1.0000x reference)
import contextlib
import math

import numpy as np
import ml_dtypes

import concourse.bass as bass
import concourse.mybir as mybir
from concourse.ap import AP
from concourse.bass_utils import run_bass_kernel_spmd

F32 = mybir.dt.float32
BF16 = mybir.dt.bfloat16
AF = mybir.ActivationFunctionType
ALU = mybir.AluOpType
AX = mybir.AxisListType
BF = ml_dtypes.bfloat16

N_CORES = 8
B_FULL, SEQ, D = 16, 2048, 1024
SEQ_PER_CORE = B_FULL // N_CORES
TOK = SEQ_PER_CORE * SEQ
NT = TOK // 128
H, DKV = 16, 64
E = 8
NEG_BIG = -1.0e30
EPS_LN = 1e-5



def _rope_perm():
    p = []
    for h in range(H):
        p.extend(h * DKV + np.arange(0, DKV, 2))
        p.extend(h * DKV + np.arange(1, DKV, 2))
    return np.asarray(p)


def _t5_bucket(delta):
    num_buckets = 16
    ret = (delta > 0).astype(np.int64) * num_buckets
    rp = np.abs(delta)
    max_exact = num_buckets // 2
    is_small = rp < max_exact
    rp_safe = np.maximum(rp, 1)
    rp_large = max_exact + (
        np.log(rp_safe.astype(np.float32) / max_exact)
        / math.log(128 / max_exact) * (num_buckets - max_exact)
    ).astype(np.int64)
    rp_large = np.minimum(rp_large, num_buckets - 1)
    return ret + np.where(is_small, rp, rp_large)


def prep_weights(inp):
    w = {}
    w["wq"] = np.ascontiguousarray(inp["L0_attn_wq"], np.float32)
    w["wk"] = np.ascontiguousarray(inp["L0_attn_wk"], np.float32)
    w["wv"] = np.ascontiguousarray(inp["L0_attn_wv"], np.float32)
    w["wo"] = np.ascontiguousarray(inp["L0_attn_wo"], np.float32)
    w["relb"] = np.ascontiguousarray(inp["L0_attn_rel_bias"], np.float32)
    w["gateT"] = np.ascontiguousarray(np.asarray(inp["L0_ffn_gate_w"]).T, np.float32)
    w["w1T"] = np.ascontiguousarray(np.asarray(inp["L0_ffn_w1"]).transpose(0, 2, 1), np.float32)
    w["w3T"] = np.ascontiguousarray(np.asarray(inp["L0_ffn_w3"]).transpose(0, 2, 1), np.float32)
    w["w2T"] = np.ascontiguousarray(np.asarray(inp["L0_ffn_w2"]).transpose(0, 2, 1), np.float32)

    perm = _rope_perm()
    wqkv = np.asarray(inp["L1_attn_Wqkv"], dtype=np.float32)
    wqkv_p = wqkv.copy()
    wqkv_p[:, 0:1024] = wqkv[:, 0:1024][:, perm]
    wqkv_p[:, 1024:2048] = wqkv[:, 1024:2048][:, perm]
    w["wqkv"] = np.ascontiguousarray(wqkv_p)
    bqkv = np.asarray(inp["L1_attn_bqkv"], dtype=np.float32)
    bqkv_p = bqkv.copy()
    bqkv_p[0:1024] = bqkv[0:1024][perm]
    bqkv_p[1024:2048] = bqkv[1024:2048][perm]
    bq3 = bqkv_p.reshape(3, 16, 64).transpose(2, 1, 0)
    w["bqkv3"] = np.ascontiguousarray(bq3.reshape(64, 48))
    w["bqkvr"] = np.ascontiguousarray(
        bqkv_p.reshape(3, 16, 64).transpose(1, 0, 2).reshape(48, 64))
    w["wo1"] = np.ascontiguousarray(inp["L1_attn_Wo"], np.float32)
    w["bo1"] = np.ascontiguousarray(np.asarray(inp["L1_attn_bo"]).reshape(1, D), np.float32)
    w["gam"] = np.ascontiguousarray(np.asarray(inp["L1_attn_gamma"]).reshape(1, D), np.float32)
    w["bet"] = np.ascontiguousarray(np.asarray(inp["L1_attn_beta"]).reshape(1, D), np.float32)
    w["routT"] = np.ascontiguousarray(np.asarray(inp["L1_ffn_router_w"]).T, np.float32)
    w["fw1"] = np.ascontiguousarray(inp["L1_ffn_w1"], np.float32)
    w["fw2"] = np.ascontiguousarray(inp["L1_ffn_w2"], np.float32)
    return w


def host_consts():
    c = {}
    c["id128"] = np.eye(128, dtype=BF)
    c["anti128"] = np.eye(128, dtype=BF)[::-1].copy()
    c["ones1x128_bf"] = np.ones((1, 128), dtype=BF)
    delta = np.arange(4095) - 2047
    bk = _t5_bucket(delta)
    oh = np.zeros((32, 4095), dtype=BF)
    oh[bk, np.arange(4095)] = 1.0
    c["onehotT"] = oh
    inv_freq = 1.0 / (10000.0 ** (np.arange(0, DKV, 2, dtype=np.float64) / DKV))
    pos = np.arange(SEQ, dtype=np.float64)
    fr = pos[:, None] * inv_freq[None, :]
    cos = np.cos(fr).T.astype(np.float32)
    sin = np.sin(fr).T.astype(np.float32)
    sc = 1.0 / math.sqrt(DKV)
    c["cs_k"] = np.concatenate([cos, cos], 0).astype(BF)
    c["sn_k"] = np.concatenate([sin, sin], 0).astype(BF)
    c["cs_q"] = (np.concatenate([cos, cos], 0) * sc).astype(BF)
    c["sn_q"] = (np.concatenate([sin, sin], 0) * sc).astype(BF)
    c["rsign"] = np.concatenate(
        [-np.ones((32, 1)), np.ones((32, 1))]).astype(np.float32)
    fb = np.zeros((128, 4), np.float32)
    fb[:, 0] = EPS_LN
    fb[:, 1] = -0.5
    fb[:, 2] = -1.0
    fb[:, 3] = 1.0
    c["fbank"] = fb
    return c



class S:
    __slots__ = ("h", "n")

    def __init__(self, h):
        self.h = h
        self.n = 0

    def inc(self, instr, k=1):
        instr.then_inc(self.h, k)
        self.n += k
        return self.n


class Ctx:
    def __init__(self, nc, stack, block):
        self.nc = nc
        self.stack = stack
        self.block = block
        self.cs = {}
        self._snum = 0
        self._uid = 0
        self.phase_sems = []

    def sem(self, stack=None):
        self._snum += 1
        h = self.nc.alloc_semaphore(f"s{self._snum}")
        self.phase_sems.append(h)
        return S(h)

    def psem(self):
        self._snum += 1
        return S(self.nc.alloc_semaphore(f"sp{self._snum}"))

    def uid(self):
        self._uid += 1
        return self._uid


def end_phase(c):
    emit_barrier(c)
    sems = list(c.phase_sems)
    c.phase_sems = []
    if not hasattr(c, "_bsem2"):
        c._bsem2 = c.psem()
        c._ep = 0
    c._ep += 1
    ep = c._ep

    def f(gp):
        c.nc._clear_and_free_semaphores(c.nc.gpsimd, sems)
        gp.nop().then_inc(c._bsem2.h, 1)
    c.block.gpsimd(f)
    for en in ("sync", "tensor", "vector", "scalar"):
        def g(eng, ep=ep):
            eng.wait_ge(c._bsem2.h, ep)
        getattr(c.block, en)(g)


def CE(eng, sem, instr):
    n = sem.inc(instr)
    eng.wait_ge(sem.h, n)
    return instr


def emit_barrier(c):
    b = c.sem()
    for en in ("sync", "tensor", "vector", "scalar", "gpsimd"):
        def f(eng, b=b):
            eng.nop().then_inc(b.h, 1)
            eng.wait_ge(b.h, 5)
        getattr(c.block, en)(f)
    b.n = 5


def load_consts(c, consts, names, stack=None):
    done = c.cs.get("_sem")
    if done is None:
        done = c.psem()
        c.cs["_sem"] = done
    handles = {}
    aps = {}
    for nm in names:
        arr = consts[nm]
        t = c.nc.inline_tensor(arr, f"c_{nm}")
        dt = BF16 if arr.dtype == BF else F32
        handles[nm] = (stack or c.stack).enter_context(
            c.nc.sbuf_tensor(f"cs_{nm}", list(arr.shape), dt))
        aps[nm] = t.ap()

    def f(sync):
        for nm in names:
            done.inc(sync.dma_start(out=handles[nm][:], in_=aps[nm]), 16)
    c.block.sync(f)
    c.cs.update(handles)
    return handles



def emit_p1(c, L, x_ap, xnT_ap, cw_ap, xnb_ap, gateT_ap):
    nc = c.nc
    u = c.uid()
    with contextlib.ExitStack() as st:
        def sb(nm, sh, dt):
            return st.enter_context(nc.sbuf_tensor(f"p1{u}_{nm}", sh, dt))

        def ps(nm, sh, dt=F32):
            return st.enter_context(nc.psum_tensor(f"p1{u}_{nm}", sh, dt))

        xbuf = [sb(f"x{i}", [128, 1024], F32) for i in range(2)]
        scr = sb("scr", [128, 1024], F32)
        stat2 = [sb(f"stat{i}", [128, 8], F32) for i in range(2)]
        xnb = [sb(f"xnb{i}", [128, 1024], BF16) for i in range(2)]
        stage = sb("stage", [128, 4096], BF16)
        cwst = sb("cwst", [128, 32], F32)
        pu2 = [sb(f"pu{i}", [128, 8], F32) for i in range(2)]
        msk = sb("msk", [128, 8], F32)
        cwr = sb("cwr", [128, 8], F32)
        rst = sb("rst", [128, 4], F32)
        gates = sb("gates", [128, 64], BF16)
        pst = [ps(f"t{i}", [128, 1024], BF16) for i in range(2)]
        psr = [ps(f"r{i}", [128, 8], F32) for i in range(2)]

        ldx = [c.sem(), c.sem()]
        ldw = c.sem()
        s_ss = c.sem()
        s_rstd = c.sem()
        s_pu = c.sem()
        s_var = c.sem()
        s_xn = c.sem()
        s_ev = c.sem()
        s_cw = c.sem()
        s_negm = c.sem()
        s_tp = c.sem()
        s_rmm = c.sem()
        stg = c.sem()
        xnbs = [c.sem(), c.sem()]
        vself = c.sem()
        sself = c.sem()

        NG = NT // 4

        @c.block.gpsimd
        def _(gp):
            src = gateT_ap.rearrange("(c p) e -> p c e", p=128)
            dst = gates.ap().rearrange("p (c e) -> p c e", c=8)
            ldw.inc(gp.dma_start(out=dst, in_=src), 16)

        @c.block.sync
        def _(sync):
            for t in range(NT):
                if t >= 2:
                    sync.wait_ge(s_xn.h, t - 1)
                    sync.wait_ge(s_ss.h, t - 1)
                ldx[t % 2].inc(sync.dma_start(
                    out=xbuf[t % 2][:], in_=x_ap[t * 128:(t + 1) * 128, :]), 16)
                if t % 4 == 3:
                    g = t // 4
                    sync.wait_ge(s_ev.h, t + 1)
                    sync.wait_ge(s_cw.h, t + 1)
                    for ch in range(8):
                        stg.inc(sync.dma_start(
                            out=xnT_ap[ch, :, g * 512:(g + 1) * 512],
                            in_=stage[:, ch * 512:(ch + 1) * 512]), 16)
                    cwv = cw_ap[g * 4:(g + 1) * 4, :, :].rearrange(
                        "t p e -> p t e")
                    stg.inc(sync.dma_start(
                        out=cwv,
                        in_=cwst.ap().rearrange("p (t e) -> p t e", t=4)), 16)
                    sync.wait_ge(stg.h, stg.n)
            if L == 1:
                sync.wait_ge(xnbs[0].h, 16 * (NT // 2))
                sync.wait_ge(xnbs[1].h, 16 * (NT // 2))

        @c.block.scalar
        def _(s):
            cse = c.cs["_sem"]
            s.wait_ge(cse.h, cse.n)
            for t in range(NT):
                stat = stat2[t % 2]
                pu = pu2[t % 2]
                s.wait_ge(ldx[t % 2].h, 16 * (t // 2 + 1))
                if t >= 2:
                    s.wait_ge(s_xn.h, t - 1)
                CE(s, sself, s.activation(scr[:], xbuf[t % 2][:], AF.Square,
                                          accum_out=stat[:, 1:2]))
                s_ss.inc(s.nop())
                s.wait_ge(s_var.h, t + 1)
                CE(s, sself, s.activation(stat[:, 4:5], stat[:, 3:4], AF.Ln,
                                          bias=c.cs["fbank"][:, 0:1]))
                CE(s, sself, s.activation(stat[:, 5:6], stat[:, 4:5], AF.Exp,
                                          scale=c.cs["fbank"][:, 1:2]))
                s_rstd.inc(s.nop())
                s.wait_ge(s_rmm.h, t + 1)
                if t >= 2:
                    s.wait_ge(s_cw.h, t - 1)
                if L == 0:
                    s.wait_ge(s_negm.h, t + 1)
                    CE(s, sself, s.activation(pu[:], psr[t % 2][:], AF.Exp,
                                              bias=stat[:, 7:8]))
                else:
                    CE(s, sself, s.activation(pu[:], psr[t % 2][:], AF.Exp,
                                              scale=c.cs["fbank"][:, 2:3]))
                s_pu.inc(s.nop())

        @c.block.vector
        def _(v):
            for t in range(NT):
                stat = stat2[t % 2]
                pu = pu2[t % 2]
                v.wait_ge(ldx[t % 2].h, 16 * (t // 2 + 1))
                CE(v, vself, v.reduce_sum(stat[:, 0:1], xbuf[t % 2][:], AX.X))
                CE(v, vself, v.tensor_scalar(stat[:, 2:3], stat[:, 0:1],
                                             1.0 / D, None, ALU.mult))
                CE(v, vself, v.tensor_tensor(stat[:, 6:7], stat[:, 2:3],
                                             stat[:, 2:3], ALU.mult))
                v.wait_ge(s_ss.h, t + 1)
                CE(v, vself, v.scalar_tensor_tensor(
                    stat[:, 3:4], stat[:, 1:2], 1.0 / D, stat[:, 6:7],
                    ALU.mult, ALU.subtract))
                s_var.inc(v.engine_nop())
                v.wait_ge(s_rstd.h, t + 1)
                if L == 1 and t >= 2:
                    v.wait_ge(xnbs[t % 2].h, 16 * (t // 2))
                CE(v, vself, v.tensor_scalar(
                    xnb[t % 2][:], xbuf[t % 2][:], stat[:, 2:3], stat[:, 5:6],
                    ALU.subtract, ALU.mult))
                s_xn.inc(v.engine_nop())
                v.wait_ge(s_tp.h, t + 1)
                if t >= 4:
                    v.wait_ge(stg.h, 144 * (t // 4))
                pv = pst[t % 2].ap().rearrange("p (c f) -> p c f", c=8)
                sv = stage.ap().rearrange("p (c g f) -> p c g f", c=8, g=4)
                CE(v, vself, v.tensor_copy(sv[:, :, t % 4, :], pv))
                s_ev.inc(v.engine_nop())
                if L == 0:
                    v.wait_ge(s_rmm.h, t + 1)
                    if t >= 2:
                        v.wait_ge(s_pu.h, t - 1)
                    CE(v, vself, v.reduce_max(stat[:, 7:8], psr[t % 2][:],
                                              AX.X, negate=True))
                    s_negm.inc(v.engine_nop())
                v.wait_ge(s_pu.h, t + 1)
                if L == 1:
                    CE(v, vself, v.tensor_scalar(pu[:], pu[:], 1.0, None,
                                                 ALU.add))
                    CE(v, vself, v.reciprocal(pu[:], pu[:]))
                CE(v, vself, v.reduce_max(rst[:, 0:1], pu[:], AX.X))
                CE(v, vself, v.tensor_scalar(msk[:], pu[:], rst[:, 0:1], None,
                                             ALU.is_ge))
                CE(v, vself, v.scalar_tensor_tensor(msk[:], msk[:], NEG_BIG,
                                                    pu[:], ALU.mult, ALU.add))
                CE(v, vself, v.reduce_max(rst[:, 1:2], msk[:], AX.X))
                CE(v, vself, v.scalar_tensor_tensor(cwr[:], pu[:], rst[:, 1:2],
                                                    pu[:], ALU.is_ge,
                                                    ALU.mult))
                CE(v, vself, v.reduce_sum(rst[:, 2:3], cwr[:], AX.X))
                if L == 1:
                    CE(v, vself, v.tensor_scalar(rst[:, 2:3], rst[:, 2:3],
                                                 1e-20, None, ALU.add))
                CE(v, vself, v.reciprocal(rst[:, 3:4], rst[:, 2:3]))
                CE(v, vself, v.tensor_scalar(
                    cwst[:, (t % 4) * 8:(t % 4) * 8 + 8], cwr[:],
                    rst[:, 3:4], None, ALU.mult))
                s_cw.inc(v.engine_nop())

        @c.block.tensor
        def _(tn):
            cse = c.cs["_sem"]
            tn.wait_ge(cse.h, cse.n)
            tn.wait_ge(ldw.h, 16)
            for t in range(NT):
                tn.wait_ge(s_xn.h, t + 1)
                if t >= 2:
                    tn.wait_ge(s_ev.h, t - 1)
                for ch in range(8):
                    ins = tn.transpose(
                        pst[t % 2][:, ch * 128:(ch + 1) * 128],
                        xnb[t % 2][:, ch * 128:(ch + 1) * 128],
                        c.cs["id128"][:])
                s_tp.inc(ins)
                tn.wait_ge(s_ev.h, t + 1)
                if t >= 2:
                    tn.wait_ge(s_pu.h, t - 1)
                    if L == 0:
                        tn.wait_ge(s_negm.h, t - 1)
                for ch in range(8):
                    ins = tn.matmul(
                        psr[t % 2][:],
                        stage[:, ch * 512 + (t % 4) * 128:
                              ch * 512 + (t % 4) * 128 + 128],
                        gates[:, ch * 8:(ch + 1) * 8],
                        start=(ch == 0), stop=(ch == 7))
                s_rmm.inc(ins)

        if L == 1:
            @c.block.gpsimd
            def _(gp):
                for t in range(NT):
                    gp.wait_ge(s_xn.h, t + 1)
                    xnbs[t % 2].inc(gp.dma_start(
                        out=xnb_ap[t * 128:(t + 1) * 128, :],
                        in_=xnb[t % 2][:]), 16)




def emit_p2(c, L, xnT_ap, cw_ap, g_ap, w1_slice, w3_slice, w2_slice, NIC):
    nc = c.nc
    u = c.uid()
    NB = 4
    has_w3 = L == 0
    act = AF.Silu if L == 0 else AF.Relu
    with contextlib.ExitStack() as st:
        def sb(nm, sh, dt):
            return st.enter_context(nc.sbuf_tensor(f"p2{L}{u}_{nm}", sh, dt))

        xnb = [sb(f"xn{i}", [128, 8 * 1024], BF16) for i in range(2)]
        cwb = [sb(f"cw{i}", [128, 64], F32) for i in range(2)]
        w1c = [sb(f"w1{i}", [128, 8 * 512], BF16) for i in range(2)]
        w3c = [sb(f"w3{i}", [128, 8 * 512], BF16) for i in range(2)] if has_w3 else None
        w2c = [sb(f"w2{i}", [128, 4 * 1024], BF16) for i in range(2)]
        s1b = [sb(f"s1{i}", [128, 1024], BF16) for i in range(2)]
        hT = sb("hT", [128, NIC * 1024], BF16)
        gacc = sb("gacc", [128, 8 * 1024], F32)
        gout = sb("gout", [128, 8 * 1024], BF16)

        ldxn = [c.sem(), c.sem()]
        ldcw = [c.sem(), c.sem()]
        ldw1 = [c.sem(), c.sem()]
        ldw3 = [c.sem(), c.sem()]
        ldw2 = [c.sem(), c.sem()]
        s_x1 = c.sem()
        s_s1 = c.sem()
        s_h = c.sem()
        s_mm2 = c.sem()
        s_ga = c.sem()
        s_gout = c.sem()
        stg_g = c.sem()
        vself = c.sem()
        sself = c.sem()

        def BE(blk, e):
            return blk * 8 + e

        ps_o = [st.enter_context(
            nc.psum_tensor(f"p2{L}{u}_po{i}", [128, 1024], F32))
            for i in range(4)]
        ps_x1 = [ps_o[0], ps_o[1]]
        ps_x3 = [ps_o[2], ps_o[3]]

        NG4 = NIC // 4

        @c.block.gpsimd
        def _(gp):
            for blk in range(NB):
                for e in range(8):
                    be = BE(blk, e)
                    for g4 in range(NG4):
                        p = g4 % 2
                        if be * NG4 + g4 >= 2:
                            prevg = be * NIC + (g4 - 2) * 4 + 4
                            gp.wait_ge(s_x1.h, prevg)
                        src1 = w1_slice(e, g4).rearrange(
                            "(c p) i -> p c i", p=128)
                        ldw1[p].inc(gp.dma_start(
                            out=w1c[p].ap().rearrange("p (c i) -> p c i", c=8),
                            in_=src1), 16)
                        if has_w3:
                            src3 = w3_slice(e, g4).rearrange(
                                "(c p) i -> p c i", p=128)
                            ldw3[p].inc(gp.dma_start(
                                out=w3c[p].ap().rearrange(
                                    "p (c i) -> p c i", c=8),
                                in_=src3), 16)
                    for shalf in range(2):
                        for g4 in range(NG4):
                            p = g4 % 2
                            gidx = (be * 2 + shalf) * NG4 + g4
                            if gidx >= 2:
                                prevg = (be * 2 + shalf) * NIC + (g4 - 2) * 4 + 4
                                gp.wait_ge(s_mm2.h, prevg)
                            src2 = w2_slice(e, g4).rearrange(
                                "(r p) d -> p r d", p=128)
                            ldw2[p].inc(gp.dma_start(
                                out=w2c[p].ap().rearrange(
                                    "p (r d) -> p r d", r=4),
                                in_=src2), 16)

        @c.block.sync
        def _(sync):
            for blk in range(NB):
                if blk >= 2:
                    sync.wait_ge(s_x1.h, (blk - 1) * 8 * NIC)
                    sync.wait_ge(s_ga.h, (blk - 1) * 16)
                for ch in range(8):
                    ldxn[blk % 2].inc(sync.dma_start(
                        out=xnb[blk % 2][:, ch * 1024:(ch + 1) * 1024],
                        in_=xnT_ap[ch, :, blk * 1024:(blk + 1) * 1024]), 16)
                cwv = cw_ap[blk * 8:(blk + 1) * 8, :, :].rearrange(
                    "t p e -> p t e")
                ldcw[blk % 2].inc(sync.dma_start(
                    out=cwb[blk % 2].ap().rearrange("p (t e) -> p t e", t=8),
                    in_=cwv), 16)
                if blk >= 1:
                    sync.wait_ge(s_gout.h, blk)
                    gv = g_ap[(blk - 1) * 1024:blk * 1024, :].rearrange(
                        "(t p) d -> p t d", p=128)
                    stg_g.inc(sync.dma_start(
                        out=gv,
                        in_=gout.ap().rearrange("p (t d) -> p t d", t=8)), 16)
                    sync.wait_ge(stg_g.h, stg_g.n)
            sync.wait_ge(s_gout.h, NB)
            gv = g_ap[(NB - 1) * 1024:NB * 1024, :].rearrange(
                "(t p) d -> p t d", p=128)
            stg_g.inc(sync.dma_start(
                out=gv,
                in_=gout.ap().rearrange("p (t d) -> p t d", t=8)), 16)
            sync.wait_ge(stg_g.h, 16 * NB)

        @c.block.tensor
        def _(tn):
            for blk in range(NB):
                tn.wait_ge(ldxn[blk % 2].h, (blk // 2 + 1) * 128)
                for e in range(8):
                    be = BE(blk, e)
                    for ic in range(NIC):
                        p = ic % 2
                        g4 = ic // 4
                        pw = g4 % 2
                        io = ic % 4
                        if io == 0:
                            tn.wait_ge(ldw1[pw].h,
                                       16 * (be * (NG4 // 2) + g4 // 2 + 1))
                            if has_w3:
                                tn.wait_ge(ldw3[pw].h,
                                           16 * (be * (NG4 // 2) + g4 // 2 + 1))
                        if ic < 2 and be >= 1:
                            tn.wait_ge(s_ga.h, be * 2)
                        if be * NIC + ic >= 2:
                            tn.wait_ge(s_h.h, be * NIC + ic - 1)
                        for half in range(2):
                            for kc in range(8):
                                ins = tn.matmul(
                                    ps_x1[p][:, half * 512:(half + 1) * 512],
                                    w1c[pw][:, kc * 512 + io * 128:
                                            kc * 512 + io * 128 + 128],
                                    xnb[blk % 2][:, kc * 1024 + half * 512:
                                                 kc * 1024 + half * 512 + 512],
                                    start=(kc == 0), stop=(kc == 7))
                        if has_w3:
                            for half in range(2):
                                for kc in range(8):
                                    ins = tn.matmul(
                                        ps_x3[p][:, half * 512:(half + 1) * 512],
                                        w3c[pw][:, kc * 512 + io * 128:
                                                kc * 512 + io * 128 + 128],
                                        xnb[blk % 2][:, kc * 1024 + half * 512:
                                                     kc * 1024 + half * 512 + 512],
                                        start=(kc == 0), stop=(kc == 7))
                        s_x1.inc(ins)
                    for shalf in range(2):
                        if be * 2 + shalf >= 1:
                            tn.wait_ge(s_ga.h, be * 2 + shalf)
                        for ic in range(NIC):
                            g4 = ic // 4
                            pw = g4 % 2
                            io = ic % 4
                            if io == 0:
                                g2 = (be * 2 + shalf) * NG4 + g4
                                tn.wait_ge(ldw2[pw].h, 16 * (g2 // 2 + 1))
                            if shalf == 0 and ic == 0:
                                tn.wait_ge(s_h.h, be * NIC + NIC)
                            for sub in range(4):
                                gsub = shalf * 4 + sub
                                for dch in range(2):
                                    ins = tn.matmul(
                                        ps_o[sub][:, dch * 512:(dch + 1) * 512],
                                        hT[:, ic * 1024 + gsub * 128:
                                           ic * 1024 + gsub * 128 + 128],
                                        w2c[pw][:, io * 1024 + dch * 512:
                                                io * 1024 + dch * 512 + 512],
                                        start=(ic == 0), stop=(ic == NIC - 1))
                            s_mm2.inc(ins)

        @c.block.scalar
        def _(sc):
            for blk in range(NB):
                for e in range(8):
                    be = BE(blk, e)
                    for ic in range(NIC):
                        p = ic % 2
                        sc.wait_ge(s_x1.h, be * NIC + ic + 1)
                        if be * NIC + ic >= 2:
                            sc.wait_ge(s_h.h, be * NIC + ic - 1)
                        CE(sc, sself, sc.activation(s1b[p][:], ps_x1[p][:],
                                                    act))
                        s_s1.inc(sc.nop())
                sc.wait_ge(s_ga.h, (blk + 1) * 16)
                if blk >= 1:
                    sc.wait_ge(stg_g.h, 16 * blk)
                for sub in range(8):
                    CE(sc, sself, sc.activation(
                        gout[:, sub * 1024:(sub + 1) * 1024],
                        gacc[:, sub * 1024:(sub + 1) * 1024], AF.Silu))
                s_gout.inc(sc.nop())

        @c.block.vector
        def _(v):
            for blk in range(NB):
                v.wait_ge(ldcw[blk % 2].h, 16 * (blk // 2 + 1))
                for e in range(8):
                    be = BE(blk, e)
                    for ic in range(NIC):
                        p = ic % 2
                        v.wait_ge(s_s1.h, be * NIC + ic + 1)
                        if e > 0 and ic == 0:
                            v.wait_ge(s_mm2.h, be * 2 * NIC)
                        if has_w3:
                            CE(v, vself, v.tensor_tensor(
                                hT[:, ic * 1024:(ic + 1) * 1024],
                                s1b[p][:], ps_x3[p][:], ALU.mult))
                        else:
                            CE(v, vself, v.tensor_tensor(
                                hT[:, ic * 1024:(ic + 1) * 1024],
                                s1b[p][:], s1b[p][:], ALU.mult))
                        s_h.inc(v.engine_nop())
                    for shalf in range(2):
                        v.wait_ge(s_mm2.h, (be * 2 + shalf + 1) * NIC)
                        if blk >= 1 and e == 0 and shalf == 0:
                            v.wait_ge(s_gout.h, blk)
                        for sub in range(4):
                            gsub = shalf * 4 + sub
                            cwsl = cwb[blk % 2][:, gsub * 8 + e:gsub * 8 + e + 1]
                            dst = gacc[:, gsub * 1024:(gsub + 1) * 1024]
                            if e == 0:
                                CE(v, vself, v.tensor_scalar(
                                    dst, ps_o[sub][:], cwsl, None, ALU.mult))
                            else:
                                CE(v, vself, v.scalar_tensor_tensor(
                                    dst, ps_o[sub][:], cwsl, dst,
                                    ALU.mult, ALU.add))
                        s_ga.inc(v.engine_nop())




def emit_gbias(c, relb_ap, gbias_ap):
    nc = c.nc
    u = c.uid()
    with contextlib.ExitStack() as st:
        if "onehotT" not in c.cs:
            load_consts(c, host_consts(), ["onehotT"], stack=st)
        relb = st.enter_context(nc.sbuf_tensor(f"p0{u}_relb", [32, 16], BF16))
        gsb = st.enter_context(nc.sbuf_tensor(f"p0{u}_g", [16, 4095], BF16))
        psg = st.enter_context(nc.psum_tensor(f"p0{u}_ps", [16, 512], F32))
        ldr = c.sem()
        s_mm = c.sem()
        s_ev = c.sem()
        stg = c.sem()

        @c.block.gpsimd
        def _(gp):
            ldr.inc(gp.dma_start(out=relb[:], in_=relb_ap), 16)

        @c.block.tensor
        def _(tn):
            cse = c.cs["_sem"]
            tn.wait_ge(cse.h, cse.n)
            tn.wait_ge(ldr.h, 16)
            for j in range(8):
                w = min(512, 4095 - j * 512)
                if j >= 1:
                    tn.wait_ge(s_ev.h, j)
                s_mm.inc(tn.matmul(psg[:, 0:w], relb[:],
                                   c.cs["onehotT"][:, j * 512:j * 512 + w],
                                   start=True, stop=True))

        @c.block.vector
        def _(v):
            for j in range(8):
                w = min(512, 4095 - j * 512)
                v.wait_ge(s_mm.h, j + 1)
                s_ev.inc(v.tensor_copy(gsb[:, j * 512:j * 512 + w],
                                       psg[:, 0:w]))

        @c.block.sync
        def _(sync):
            sync.wait_ge(s_ev.h, 8)
            stg.inc(sync.dma_start(out=gbias_ap, in_=gsb[:]), 16)
            sync.wait_ge(stg.h, 16)




def emit_p3(c, L, xnT_ap, ctxT_ap, wq_ap=None, wk_ap=None, wv_ap=None,
            gbias_ap=None, wqkv_ap=None, bqkv3_ap=None, bqkvr_ap=None):
    nc = c.nc
    u = c.uid()
    pre = f"p3{L}{u}"
    with contextlib.ExitStack() as st:
        if L == 1 and "cs_q" not in c.cs:
            load_consts(c, host_consts(),
                        ["cs_q", "sn_q", "cs_k", "sn_k", "rsign"], stack=st)
        def sb(nm, sh, dt):
            return st.enter_context(nc.sbuf_tensor(f"{pre}_{nm}", sh, dt))

        PF = st.enter_context(nc.psum_tensor(f"{pre}_PF", [128, 2048], F32))
        PS0 = st.enter_context(nc.psum_tensor(f"{pre}_PS0", [128, 512], F32))
        PS1 = st.enter_context(nc.psum_tensor(f"{pre}_PS1", [128, 512], F32))
        PSC = st.enter_context(nc.psum_tensor(f"{pre}_PSC", [128, 512], F32))
        PT0 = st.enter_context(nc.psum_tensor(f"{pre}_PT0", [128, 512], F32))
        xnq = sb("xnq", [128, 8 * 4096], BF16)
        wqh = sb("wqh", [128, 512], BF16)
        wkh = sb("wkh", [128, 512], BF16)
        wvh = sb("wvh", [128, 512], BF16)
        qT2 = [sb(f"qT{i}", [64, 2048], BF16) for i in range(2)]
        kT2 = [sb(f"kT{i}", [64, 2048], BF16) for i in range(2)]
        vaug = sb("vaug", [128, 16 * 65], BF16)
        prb = [sb(f"prb{i}", [128, 512], BF16) for i in range(2)]
        czb = sb("czb", [65, 2048], BF16)
        mA = sb("mA", [128, 32], F32)
        fold = sb("fold", [128, 8], F32)
        ftile = sb("ftile", [32, 32], F32)
        ftileT = sb("ftileT", [32, 32], F32)
        g1 = sb("g1", [1, 2], F32)
        negMbf = sb("negMbf", [1, 1], BF16)
        negM = sb("negM", [128, 1], F32)
        ctok = [sb(f"ctok{i}", [128, 64], BF16) for i in range(2)]
        cstg = [sb(f"cstg{i}", [128, 2048], BF16) for i in range(2)]
        if L == 0:
            lqsb = sb("lq", [128, 31 * 128], BF16)
            barr2 = [sb(f"barr{i}", [128, 31 * 128], BF16) for i in range(2)]
        else:
            qb = sb("qb", [64, 2048], BF16)
            kb = sb("kb", [64, 2048], BF16)
            uu = sb("uu", [64, 2048], BF16)
            usw = sb("usw", [64, 2048], BF16)
            qcs = sb("qcs", [64, 2048], BF16)
            bq3 = sb("bq3", [64, 48], F32)
            bvr = sb("bvr", [1, 64], BF16)

        ldxn = c.sem()
        ldh = c.sem()
        ldb3 = c.sem()
        s_bbm = c.sem()
        s_bb = c.sem()
        s_qk = c.sem()
        s_qbk = c.sem()
        s_qke = c.sem()
        s_vt = c.sem()
        s_vev = c.sem()
        s_pa = c.sem()
        s_par = c.sem()
        s_fold = c.sem()
        s_bc = c.sem()
        s_nm = c.sem()
        s_qkmm = c.sem()
        s_exp = c.sem()
        s_avk = c.sem()
        s_cze = c.sem()
        s_tr1 = c.sem()
        s_rz = c.sem()
        s_tr2 = c.sem()
        s_cst = c.sem()
        stgc = [c.sem(), c.sem()]
        vself = c.sem()
        sself = c.sem()

        NH_DMA = 4

        @c.block.sync
        def _(sync):
            for ch in range(8):
                ldxn.inc(sync.dma_start(
                    out=xnq[:, ch * 4096:(ch + 1) * 4096],
                    in_=xnT_ap[ch, :, :]), 16)
            for h in range(1, 16, 2):
                for sq in range(2):
                    sync.wait_ge(s_cst.h, 16 * (h * 2 + sq + 1))
                    stgc[sq].inc(sync.dma_start(
                        out=ctxT_ap[h // 2, :, sq * 2048:(sq + 1) * 2048],
                        in_=cstg[sq][:]), 16)
                    sync.wait_ge(stgc[sq].h, stgc[sq].n)

        @c.block.gpsimd
        def _(gp):
            if L == 1:
                ldb3.inc(gp.dma_start(out=bq3[:], in_=bqkv3_ap), 16)
            for h in range(16):
                if h >= 1:
                    gp.wait_ge(s_vt.h, (h * 2) * 16)
                    if L == 0:
                        gp.wait_ge(s_bb.h, 31 * h)
                if L == 0:
                    srcs = [wq_ap[:, h * 64:(h + 1) * 64],
                            wk_ap[:, h * 64:(h + 1) * 64],
                            wv_ap[:, h * 64:(h + 1) * 64]]
                else:
                    srcs = [wqkv_ap[:, h * 64:(h + 1) * 64],
                            wqkv_ap[:, 1024 + h * 64:1024 + (h + 1) * 64],
                            wqkv_ap[:, 2048 + h * 64:2048 + (h + 1) * 64]]
                for dst, src in zip((wqh, wkh, wvh), srcs):
                    ldh.inc(gp.dma_start(
                        out=dst.ap().rearrange("p (c o) -> p c o", c=8),
                        in_=src.rearrange("(c p) o -> p c o", p=128)), 16)
                if L == 0:
                    src = AP(gbias_ap.tensor, h * 4095,
                             [[1, 128], [128, 31], [1, 128]])
                    ldh.inc(gp.dma_start(
                        out=lqsb.ap().rearrange("p (i k) -> p i k", i=31),
                        in_=src), 16)
                else:
                    ldh.inc(gp.dma_start(
                        out=bvr[:], in_=bqkvr_ap[h * 3 + 2:h * 3 + 3, :]), 16)

        @c.block.tensor
        def _(tn):
            cse = c.cs["_sem"]
            tn.wait_ge(cse.h, cse.n)
            tn.wait_ge(ldxn.h, 128)
            for h in range(16):
                for sq in range(2):
                    hs = h * 2 + sq
                    tn.wait_ge(ldh.h, 16 * NH_DMA * (h + 1))
                    if hs >= 1:
                        tn.wait_ge(s_par.h, 32 * hs)
                    do_build = (L == 0 and sq == 0)
                    if do_build:
                        bslots = [PS0, PS1, PSC, PT0]
                        if hs >= 1:
                            tn.wait_ge(s_cst.h, 16 * hs)
                            tn.wait_ge(s_exp.h, 64 * hs)
                            tn.wait_ge(s_cze.h, 4 * hs)
                            tn.wait_ge(s_nm.h, hs)
                    bi = 0
                    for nsl in range(4):
                        for kc in range(8):
                            if do_build and bi < 31:
                                if h * 31 + bi >= 4:
                                    tn.wait_ge(s_bb.h, h * 31 + bi - 3)
                                s_bbm.inc(tn.matmul(
                                    bslots[bi % 4][:, 0:128],
                                    lqsb[:, bi * 128:(bi + 1) * 128],
                                    c.cs["anti128"][:], start=True, stop=True))
                                bi += 1
                            ins = tn.matmul(
                                PF[0:64, nsl * 512:nsl * 512 + 512],
                                wqh[:, kc * 64:(kc + 1) * 64],
                                xnq[:, kc * 4096 + sq * 2048 + nsl * 512:
                                    kc * 4096 + sq * 2048 + nsl * 512 + 512],
                                start=(kc == 0), stop=(kc == 7))
                    for nsl in range(4):
                        for kc in range(8):
                            ins = tn.matmul(
                                PF[64:128, nsl * 512:nsl * 512 + 512],
                                wkh[:, kc * 64:(kc + 1) * 64],
                                xnq[:, kc * 4096 + sq * 2048 + nsl * 512:
                                    kc * 4096 + sq * 2048 + nsl * 512 + 512],
                                start=(kc == 0), stop=(kc == 7),
                                tile_position=(0, 64))
                    s_qk.inc(ins)
                    tn.wait_ge(s_qke.h, hs + 1)
                    for kt in range(16):
                        pv = PF[:, (kt % 2) * 512:(kt % 2) * 512 + 64]
                        if kt >= 2:
                            tn.wait_ge(s_vev.h, hs * 16 + kt - 1)
                        if L == 1:
                            tn.matmul(pv, c.cs["ones1x128_bf"][:], bvr[:],
                                      start=True, stop=False)
                        for kc in range(8):
                            ins = tn.matmul(
                                pv,
                                xnq[:, kc * 4096 + sq * 2048 + kt * 128:
                                    kc * 4096 + sq * 2048 + kt * 128 + 128],
                                wvh[:, kc * 64:(kc + 1) * 64],
                                start=(kc == 0 and L == 0), stop=(kc == 7))
                        s_vt.inc(ins)
                    qT = qT2[hs % 2]
                    kT = kT2[hs % 2]
                    if L == 0:
                        barr = barr2[h % 2]
                    for j in range(32):
                        kt, qh = j // 2, j % 2
                        if j == 0:
                            tn.wait_ge(s_vev.h, hs * 16 + 16)
                        if j >= 2:
                            tn.wait_ge(s_par.h, hs * 32 + j - 1)
                        base = (j % 2) * 1024
                        for nsl in range(2):
                            ins = tn.matmul(
                                PF[:, base + nsl * 512:base + nsl * 512 + 512],
                                kT[:, kt * 128:(kt + 1) * 128],
                                qT[:, qh * 1024 + nsl * 512:
                                   qh * 1024 + nsl * 512 + 512],
                                start=True, stop=True)
                        s_pa.inc(ins)
                    tn.wait_ge(s_fold.h, hs + 1)
                    if hs >= 1:
                        tn.wait_ge(s_exp.h, 64 * hs)
                        tn.wait_ge(s_nm.h, hs)
                    if L == 0:
                        tn.wait_ge(s_bb.h, 31 * (h + 1))
                    s_bc.inc(tn.matmul(PS0[:, 0:1],
                                       c.cs["ones1x128_bf"][:], negMbf[:],
                                       start=True, stop=True))
                    for qch in range(4):
                        for kt in range(16):
                            gk = hs * 64 + qch * 16 + kt
                            if gk >= 2:
                                tn.wait_ge(s_exp.h, gk - 1)
                            if kt == 0:
                                tn.wait_ge(s_cze.h, hs * 4 + qch)
                                if qch == 0:
                                    tn.wait_ge(s_nm.h, hs + 1)
                                    if L == 0:
                                        tn.wait_ge(s_bb.h, 31 * (h + 1))
                            psd = PS0 if kt % 2 == 0 else PS1
                            ins = tn.matmul(
                                psd[:],
                                kT[:, kt * 128:(kt + 1) * 128],
                                qT[:, qch * 512:(qch + 1) * 512],
                                start=True, stop=(L == 1))
                            if L == 0:
                                j0 = 15 - kt + 4 * qch
                                ins = tn.matmul(
                                    psd[:],
                                    c.cs["id128"][:],
                                    barr[:, 128 * j0:128 * j0 + 512],
                                    start=False, stop=True)
                            s_qkmm.inc(ins)
                            if kt >= 1:
                                tn.wait_ge(s_exp.h, gk)
                                if kt == 1 and qch == 0 and hs >= 1:
                                    tn.wait_ge(s_cst.h, 16 * hs)
                                s_avk.inc(tn.matmul(
                                    PSC[0:65, :],
                                    vaug[:, (kt - 1) * 65:(kt - 1) * 65 + 65],
                                    prb[(kt - 1) % 2][:],
                                    start=(kt == 1), stop=False))
                        gk15 = hs * 64 + qch * 16 + 15
                        tn.wait_ge(s_exp.h, gk15 + 1)
                        s_avk.inc(tn.matmul(
                            PSC[0:65, :],
                            vaug[:, 15 * 65:15 * 65 + 65],
                            prb[1][:], start=False, stop=True))
                    for i in range(16):
                        trv = (PSC if i % 2 == 0 else PT0).ap().bitcast(BF16)
                        if i == 0:
                            tn.wait_ge(s_cze.h, hs * 4 + 4)
                        if hs * 16 + i >= 2:
                            tn.wait_ge(s_cst.h, hs * 16 + i - 1)
                        s_tr1.inc(tn.transpose(
                            trv[:, 0:65],
                            czb[:, i * 128:(i + 1) * 128],
                            c.cs["id128"][0:65, 0:65]))
                        tn.wait_ge(s_rz.h, hs * 16 + i + 1)
                        s_tr2.inc(tn.transpose(
                            trv[0:64, 512:640],
                            ctok[i % 2][:], c.cs["id128"][:]))

        @c.block.scalar
        def _(sc):
            cse = c.cs["_sem"]
            sc.wait_ge(cse.h, cse.n)
            for h in range(16):
                for sq in range(2):
                    hs = h * 2 + sq
                    if L == 0:
                        sc.wait_ge(s_qk.h, hs + 1)
                        if hs >= 2:
                            sc.wait_ge(s_qkmm.h, 64 * (hs - 1))
                        CE(sc, sself, sc.activation(qT2[hs % 2][:],
                                                    PF[0:64, :], AF.Copy))
                        CE(sc, sself, sc.activation(kT2[hs % 2][:],
                                                    PF[64:128, :], AF.Copy))
                        s_qke.inc(sc.nop())
                    else:
                        sc.wait_ge(s_qk.h, hs + 1)
                        sc.wait_ge(ldb3.h, 16)
                        if hs >= 1:
                            sc.wait_ge(s_qke.h, hs)
                        CE(sc, s_qbk, sc.activation(
                            qb[:], PF[0:64, :], AF.Identity,
                            bias=bq3[:, h * 3:h * 3 + 1]))
                        CE(sc, s_qbk, sc.activation(
                            kb[:], PF[64:128, :], AF.Identity,
                            bias=bq3[:, h * 3 + 1:h * 3 + 2]))
                    for qch in range(4):
                        for kt in range(16):
                            gk = hs * 64 + qch * 16 + kt
                            sc.wait_ge(s_qkmm.h, gk + 1)
                            if gk >= 2:
                                sc.wait_ge(s_avk.h, gk - 1)
                            if qch == 0 and kt == 0:
                                sc.wait_ge(s_nm.h, hs + 1)
                            s_exp.inc(sc.activation(
                                prb[kt % 2][:],
                                (PS0 if kt % 2 == 0 else PS1)[:],
                                AF.Exp, bias=negM[:]))
                        sc.wait_ge(s_avk.h, hs * 64 + (qch + 1) * 16)
                        if qch == 0:
                            sc.wait_ge(s_tr1.h, 16 * hs)
                        s_cze.inc(sc.activation(
                            czb[:, qch * 512:(qch + 1) * 512],
                            PSC[0:65, :], AF.Copy))
                    for i in range(16):
                        trv = (PSC if i % 2 == 0 else PT0).ap().bitcast(BF16)
                        sc.wait_ge(s_tr2.h, hs * 16 + i + 1)
                        if i == 0 and h % 2 == 0:
                            sc.wait_ge(stgc[sq].h, 16 * (h // 2))
                        s_cst.inc(sc.activation(
                            cstg[sq][(h % 2) * 64:(h % 2) * 64 + 64,
                                     i * 128:(i + 1) * 128],
                            trv[0:64, 512:640], AF.Copy))

        @c.block.vector
        def _(v):
            CE(v, vself, v.memset(vaug[:], 1.0))
            CE(v, vself, v.memset(ftile[:], 0.0))
            for h in range(16):
                for sq in range(2):
                    hs = h * 2 + sq
                    if L == 0 and sq == 0:
                        for i in range(31):
                            v.wait_ge(s_bbm.h, h * 31 + i + 1)
                            if h >= 2 and i == 0:
                                v.wait_ge(s_qkmm.h, 64 * (2 * h - 2))
                            CE(v, s_bb, v.tensor_copy(
                                barr2[h % 2][:, (30 - i) * 128:
                                             (30 - i) * 128 + 128],
                                [PS0, PS1, PSC, PT0][i % 4][:, 0:128]))
                    if L == 1:
                        for (tin, tout, csn, snn) in (
                                (qb, qT2[hs % 2], "cs_q", "sn_q"),
                                (kb, kT2[hs % 2], "cs_k", "sn_k")):
                            which = 0 if tin is qb else 1
                            v.wait_ge(s_qbk.h, hs * 2 + which + 1)
                            if hs >= 2 and which == 0:
                                v.wait_ge(s_qkmm.h, 64 * (hs - 1))
                            CE(v, vself, v.tensor_tensor(
                                uu[:], tin[:], c.cs[snn][:], ALU.mult))
                            CE(v, vself, v.tensor_copy(usw[0:32, :],
                                                       uu[32:64, :]))
                            CE(v, vself, v.tensor_copy(usw[32:64, :],
                                                       uu[0:32, :]))
                            CE(v, vself, v.tensor_tensor(
                                qcs[:], tin[:], c.cs[csn][:], ALU.mult))
                            CE(v, vself, v.scalar_tensor_tensor(
                                tout[:], usw[:], c.cs["rsign"][:], qcs[:],
                                ALU.mult, ALU.add))
                        s_qke.inc(v.engine_nop())
                    for kt in range(16):
                        v.wait_ge(s_vt.h, hs * 16 + kt + 1)
                        if kt == 0 and hs >= 1:
                            v.wait_ge(s_avk.h, 64 * hs)
                        s_vev.inc(v.tensor_copy(
                            vaug[:, kt * 65:kt * 65 + 64],
                            PF[:, (kt % 2) * 512:(kt % 2) * 512 + 64]))
                    for j in range(32):
                        v.wait_ge(s_pa.h, hs * 32 + j + 1)
                        CE(v, s_par, v.reduce_max(
                            mA[:, j:j + 1],
                            PF[:, (j % 2) * 1024:(j % 2) * 1024 + 1024],
                            AX.X))
                    if hs >= 1:
                        v.wait_ge(s_bc.h, hs)
                    CE(v, vself, v.reduce_max(fold[:, 0:1], mA[:], AX.X))
                    CE(v, vself, v.tensor_copy(fold[0:64, 1:2],
                                               fold[64:128, 0:1]))
                    CE(v, vself, v.tensor_tensor(fold[0:64, 2:3],
                                                 fold[0:64, 0:1],
                                                 fold[0:64, 1:2], ALU.max))
                    CE(v, vself, v.tensor_copy(fold[0:32, 3:4],
                                               fold[32:64, 2:3]))
                    CE(v, vself, v.tensor_tensor(ftile[0:32, 0:1],
                                                 fold[0:32, 2:3],
                                                 fold[0:32, 3:4], ALU.max))
                    CE(v, vself, v.transpose(ftileT[0:32, 0:32],
                                             ftile[0:32, 0:32]))
                    CE(v, vself, v.reduce_max(g1[:, 0:1], ftileT[0:1, 0:32],
                                              AX.X))
                    CE(v, vself, v.tensor_scalar(negMbf[:], g1[:, 0:1], -1.0,
                                                 None, ALU.mult))
                    s_fold.inc(v.engine_nop())
                    v.wait_ge(s_bc.h, hs + 1)
                    CE(v, vself, v.tensor_copy(negM[:], PS0[:, 0:1]))
                    s_nm.inc(v.engine_nop())
                    for i in range(16):
                        trv = (PSC if i % 2 == 0 else PT0).ap().bitcast(BF16)
                        v.wait_ge(s_tr1.h, hs * 16 + i + 1)
                        CE(v, vself, v.reciprocal(fold[:, 4:5],
                                                  trv[:, 64:65]))
                        CE(v, s_rz, v.tensor_scalar(
                            ctok[i % 2][:], trv[:, 0:64],
                            fold[:, 4:5], None, ALU.mult))




def emit_p4(c, L, ctxT_ap, g_ap, x_ap, out_ap, wo_ap,
            bo_ap=None, gam_ap=None, bet_ap=None, xnb_ap=None):
    nc = c.nc
    u = c.uid()
    pre = f"p4{L}{u}"
    with contextlib.ExitStack() as st:
        def sb(nm, sh, dt):
            return st.enter_context(nc.sbuf_tensor(f"{pre}_{nm}", sh, dt))

        ctx_sb = sb("ctx", [128, 8 * 4096], BF16)
        wo_sb = sb("wo", [128, 8 * 1024], BF16)
        xb = [sb(f"x{i}", [128, 1024], F32) for i in range(2)]
        gb = [sb(f"g{i}", [128, 1024], BF16) for i in range(2)]
        t1 = sb("t1", [128, 1024], F32)
        xo = [sb(f"xo{i}", [128, 1024], F32) for i in range(2)]
        ps_s = [st.enter_context(
            nc.psum_tensor(f"{pre}_ps{i}", [128, 1024], F32)) for i in range(2)]
        if L == 1:
            xnb = [sb(f"xnb{i}", [128, 1024], BF16) for i in range(2)]
            bo_sb = sb("bo", [1, 1024], BF16)
            gamF = sb("gamF", [128, 1024], F32)
            betF = sb("betF", [128, 1024], F32)
            gam1 = sb("gam1", [1, 1024], F32)
            bet1 = sb("bet1", [1, 1024], F32)
            ones1 = sb("ones1", [1, 128], F32)
            hsb = sb("h", [128, 1024], F32)
            stat = sb("stat", [128, 8], F32)
            scr = sb("scr", [128, 1024], F32)
            ps_gb = st.enter_context(
                nc.psum_tensor(f"{pre}_psgb", [128, 1024], F32))

        ldc = c.sem()
        ldx2 = [c.sem(), c.sem()]
        ldg2 = [c.sem(), c.sem()]
        ldn2 = [c.sem(), c.sem()]
        s_gb = c.sem()
        s_mm = c.sem()
        s_ss2 = c.sem()
        s_var2 = c.sem()
        s_rstd2 = c.sem()
        s_hln = c.sem()
        s_out = c.sem()
        sto = [c.sem(), c.sem()]
        vself = c.sem()
        sself = c.sem()

        @c.block.gpsimd
        def _(gp):
            for ch in range(8):
                ldc.inc(gp.dma_start(
                    out=ctx_sb[:, ch * 4096:(ch + 1) * 4096],
                    in_=ctxT_ap[ch, :, :]), 16)
            ldc.inc(gp.dma_start(
                out=wo_sb.ap().rearrange("p (cc d) -> p cc d", cc=8),
                in_=wo_ap.rearrange("(cc p) d -> p cc d", p=128)), 16)
            if L == 1:
                ldc.inc(gp.dma_start(out=bo_sb[:], in_=bo_ap), 16)
                ldc.inc(gp.dma_start(out=gam1[:], in_=gam_ap), 16)
                ldc.inc(gp.dma_start(out=bet1[:], in_=bet_ap), 16)

        if L == 1:
            s_ones = c.sem()
            s_gmm = c.sem()

            @c.block.vector
            def _(v):
                CE(v, vself, v.memset(ones1[:], 1.0))
                s_ones.inc(v.engine_nop())
                v.wait_ge(s_gmm.h, 1)
                CE(v, vself, v.tensor_copy(gamF[:], ps_gb[:]))
                s_gb.inc(v.engine_nop())
                v.wait_ge(s_gmm.h, 2)
                CE(v, vself, v.tensor_copy(betF[:], ps_gb[:]))
                s_gb.inc(v.engine_nop())

            @c.block.tensor
            def _(tn):
                tn.wait_ge(ldc.h, 16 * 12)
                tn.wait_ge(s_ones.h, 1)
                for dch in range(2):
                    ins = tn.matmul(ps_gb[:, dch * 512:(dch + 1) * 512],
                                    ones1[:],
                                    gam1[:, dch * 512:(dch + 1) * 512],
                                    start=True, stop=True)
                s_gmm.inc(ins)
                tn.wait_ge(s_gb.h, 1)
                for dch in range(2):
                    ins = tn.matmul(ps_gb[:, dch * 512:(dch + 1) * 512],
                                    ones1[:],
                                    bet1[:, dch * 512:(dch + 1) * 512],
                                    start=True, stop=True)
                s_gmm.inc(ins)

        @c.block.sync
        def _(sync):
            for t in range(NT):
                if t >= 2:
                    sync.wait_ge(s_out.h, t - 1)
                ldx2[t % 2].inc(sync.dma_start(
                    out=xb[t % 2][:], in_=x_ap[t * 128:(t + 1) * 128, :]), 16)
                ldg2[t % 2].inc(sync.dma_start(
                    out=gb[t % 2][:], in_=g_ap[t * 128:(t + 1) * 128, :]), 16)
                if L == 1:
                    ldn2[t % 2].inc(sync.dma_start(
                        out=xnb[t % 2][:],
                        in_=xnb_ap[t * 128:(t + 1) * 128, :]), 16)
                if t >= 1:
                    sync.wait_ge(s_out.h, t)
                    sto[(t - 1) % 2].inc(sync.dma_start(
                        out=out_ap[(t - 1) * 128:t * 128, :],
                        in_=xo[(t - 1) % 2][:]), 16)
                    sync.wait_ge(sto[(t - 1) % 2].h, sto[(t - 1) % 2].n)
            sync.wait_ge(s_out.h, NT)
            sto[(NT - 1) % 2].inc(sync.dma_start(
                out=out_ap[(NT - 1) * 128:NT * 128, :],
                in_=xo[(NT - 1) % 2][:]), 16)
            sync.wait_ge(sto[(NT - 1) % 2].h, sto[(NT - 1) % 2].n)

        @c.block.tensor
        def _(tn):
            cse = c.cs["_sem"]
            tn.wait_ge(cse.h, cse.n)
            tn.wait_ge(ldc.h, ldc.n)
            if L == 1:
                tn.wait_ge(s_gb.h, 2)
            for t in range(NT):
                if t >= 2:
                    tn.wait_ge(s_out.h, t - 1)
                for dch in range(2):
                    pss = ps_s[t % 2][:, dch * 512:(dch + 1) * 512]
                    if L == 1:
                        tn.matmul(pss, c.cs["ones1x128_bf"][:],
                                  bo_sb[:, dch * 512:(dch + 1) * 512],
                                  start=True, stop=False)
                    for ch in range(8):
                        ins = tn.matmul(
                            pss,
                            ctx_sb[:, ch * 4096 + t * 128:
                                   ch * 4096 + t * 128 + 128],
                            wo_sb[:, ch * 1024 + dch * 512:
                                  ch * 1024 + dch * 512 + 512],
                            start=(ch == 0 and L == 0), stop=(ch == 7))
                s_mm.inc(ins)

        if L == 0:
            @c.block.vector
            def _(v):
                for t in range(NT):
                    v.wait_ge(s_mm.h, t + 1)
                    v.wait_ge(ldg2[t % 2].h, 16 * (t // 2 + 1))
                    v.wait_ge(ldx2[t % 2].h, 16 * (t // 2 + 1))
                    if t >= 2:
                        v.wait_ge(sto[t % 2].h, 16 * (t // 2))
                    CE(v, vself, v.tensor_tensor(t1[:], ps_s[t % 2][:],
                                                 gb[t % 2][:], ALU.mult))
                    CE(v, vself, v.tensor_tensor(xo[t % 2][:], xb[t % 2][:],
                                                 t1[:], ALU.add))
                    s_out.inc(v.engine_nop())
        else:
            @c.block.scalar
            def _(sc):
                cse = c.cs["_sem"]
                sc.wait_ge(cse.h, cse.n)
                for t in range(NT):
                    sc.wait_ge(s_hln.h, t + 1)
                    CE(sc, sself, sc.activation(scr[:], hsb[:], AF.Square,
                                                accum_out=stat[:, 1:2]))
                    s_ss2.inc(sc.nop())
                    sc.wait_ge(s_var2.h, t + 1)
                    CE(sc, sself, sc.activation(stat[:, 4:5], stat[:, 3:4],
                                                AF.Ln,
                                                bias=c.cs["fbank"][:, 0:1]))
                    CE(sc, sself, sc.activation(stat[:, 5:6], stat[:, 4:5],
                                                AF.Exp,
                                                scale=c.cs["fbank"][:, 1:2]))
                    s_rstd2.inc(sc.nop())

            @c.block.vector
            def _(v):
                for t in range(NT):
                    v.wait_ge(s_mm.h, t + 1)
                    v.wait_ge(ldn2[t % 2].h, 16 * (t // 2 + 1))
                    CE(v, vself, v.tensor_tensor(hsb[:], xnb[t % 2][:],
                                                 ps_s[t % 2][:], ALU.add))
                    s_hln.inc(v.engine_nop())
                    CE(v, vself, v.reduce_sum(stat[:, 0:1], hsb[:], AX.X))
                    CE(v, vself, v.tensor_scalar(stat[:, 2:3], stat[:, 0:1],
                                                 1.0 / D, None, ALU.mult))
                    CE(v, vself, v.tensor_tensor(stat[:, 6:7], stat[:, 2:3],
                                                 stat[:, 2:3], ALU.mult))
                    v.wait_ge(s_ss2.h, t + 1)
                    CE(v, vself, v.scalar_tensor_tensor(
                        stat[:, 3:4], stat[:, 1:2], 1.0 / D, stat[:, 6:7],
                        ALU.mult, ALU.subtract))
                    s_var2.inc(v.engine_nop())
                    v.wait_ge(s_rstd2.h, t + 1)
                    CE(v, vself, v.tensor_scalar(t1[:], hsb[:], stat[:, 2:3],
                                                 stat[:, 5:6], ALU.subtract,
                                                 ALU.mult))
                    CE(v, vself, v.tensor_tensor(t1[:], t1[:], gamF[:],
                                                 ALU.mult))
                    CE(v, vself, v.tensor_tensor(t1[:], t1[:], betF[:],
                                                 ALU.add))
                    v.wait_ge(ldg2[t % 2].h, 16 * (t // 2 + 1))
                    v.wait_ge(ldx2[t % 2].h, 16 * (t // 2 + 1))
                    if t >= 2:
                        v.wait_ge(sto[t % 2].h, 16 * (t // 2))
                    CE(v, vself, v.tensor_tensor(t1[:], t1[:], gb[t % 2][:],
                                                 ALU.mult))
                    CE(v, vself, v.tensor_tensor(xo[t % 2][:], xb[t % 2][:],
                                                 t1[:], ALU.add))
                    s_out.inc(v.engine_nop())



SCRATCH = {
    "xnT0": ([8, 128, TOK], BF16),
    "xnT1": ([8, 128, TOK], BF16),
    "xnb1": ([TOK, D], BF16),
    "cw0": ([NT, 128, 8], F32),
    "cw1": ([NT, 128, 8], F32),
    "g0": ([TOK, D], BF16),
    "g1": ([TOK, D], BF16),
    "ctxT0": ([8, 128, TOK], BF16),
    "ctxT1": ([8, 128, TOK], BF16),
    "xmid": ([TOK, D], F32),
    "gbias": ([16, 4095], BF16),
}

WEIGHT_SPECS = {
    "wq": ([D, D], F32), "wk": ([D, D], F32), "wv": ([D, D], F32),
    "wo": ([D, D], F32), "relb": ([32, 16], F32), "gateT": ([D, 8], F32),
    "w1T": ([8, D, 4096], F32), "w3T": ([8, D, 4096], F32),
    "w2T": ([8, 4096, D], F32),
    "wqkv": ([D, 3 * D], F32), "bqkv3": ([64, 48], F32),
    "bqkvr": ([48, 64], F32),
    "wo1": ([D, D], F32), "bo1": ([1, D], F32),
    "gam": ([1, D], F32), "bet": ([1, D], F32), "routT": ([D, 8], F32),
    "fw1": ([D, 8192], F32), "fw2": ([8192, D], F32),
}

CONST_NAMES = ["id128", "anti128", "ones1x128_bf", "onehotT",
               "cs_k", "sn_k", "cs_q", "sn_q", "rsign"]


def _mkprog(emit, ext_in, ext_out, consts_needed):
    nc = bass.Bass("TRN2", target_bir_lowering=False)
    aps = {}
    for nm in ext_in:
        sh, dt = WEIGHT_SPECS.get(nm) or SCRATCH.get(nm) or (([TOK, D], F32) if nm in ("x", "y") else None)
        aps[nm] = nc.dram_tensor(nm, sh, dt, kind="ExternalInput").ap()
    for nm in ext_out:
        sh, dt = SCRATCH.get(nm) or (([TOK, D], F32) if nm == "y" else None)
        aps[nm] = nc.dram_tensor(nm, sh, dt, kind="ExternalOutput").ap()
    consts = host_consts()
    with contextlib.ExitStack() as stack:
        block = stack.enter_context(nc.Block())
        c = Ctx(nc, stack, block)
        load_consts(c, consts, consts_needed)
        emit(c, aps)

        cse = c.cs["_sem"]

        def fin(sync):
            sync.wait_ge(cse.h, cse.n)
        c.block.sync(fin)
    return nc


def build_p1(L):
    ext_in = ["x", "gateT" if L == 0 else "routT"]
    ext_out = [f"xnT{L}", f"cw{L}"] + (["xnb1"] if L == 1 else [])
    aps_x = "x"

    def emit(c, aps):
        emit_p1(c, L, aps["x"],
                aps[f"xnT{L}"], aps[f"cw{L}"],
                aps.get("xnb1"), aps["gateT" if L == 0 else "routT"])
    return _mkprog(emit, ext_in, ext_out, ["id128", "fbank"])


def build_p2(L):
    if L == 0:
        ext_in = ["xnT0", "cw0", "w1T", "w3T", "w2T"]
        ext_out = ["g0"]

        def emit(c, aps):
            emit_p2(c, 0, aps["xnT0"], aps["cw0"], aps["g0"],
                    lambda e, g: aps["w1T"][e, :, g * 512:(g + 1) * 512],
                    lambda e, g: aps["w3T"][e, :, g * 512:(g + 1) * 512],
                    lambda e, g: aps["w2T"][e, g * 512:(g + 1) * 512, :],
                    NIC=32)
        return _mkprog(emit, ext_in, ext_out, ["fbank"])
    else:
        ext_in = ["xnT1", "cw1", "fw1", "fw2"]
        ext_out = ["g1"]

        def emit(c, aps):
            emit_p2(c, 1, aps["xnT1"], aps["cw1"], aps["g1"],
                    lambda e, g: aps["fw1"][:, e * 1024 + g * 512:
                                            e * 1024 + (g + 1) * 512],
                    None,
                    lambda e, g: aps["fw2"][e * 1024 + g * 512:
                                            e * 1024 + (g + 1) * 512, :],
                    NIC=8)
        return _mkprog(emit, ext_in, ext_out, ["fbank"])


def build_p0():
    def emit(c, aps):
        emit_gbias(c, aps["relb"], aps["gbias"])
    return _mkprog(emit, ["relb"], ["gbias"], ["onehotT"])


def build_p3(L):
    if L == 0:
        ext_in = ["xnT0", "wq", "wk", "wv", "gbias"]
        ext_out = ["ctxT0"]

        def emit(c, aps):
            emit_p3(c, 0, aps["xnT0"], aps["ctxT0"], wq_ap=aps["wq"],
                    wk_ap=aps["wk"], wv_ap=aps["wv"], gbias_ap=aps["gbias"])
        return _mkprog(emit, ext_in, ext_out,
                       ["id128", "anti128", "ones1x128_bf"])
    else:
        ext_in = ["xnT1", "wqkv", "bqkv3", "bqkvr"]
        ext_out = ["ctxT1"]

        def emit(c, aps):
            emit_p3(c, 1, aps["xnT1"], aps["ctxT1"], wqkv_ap=aps["wqkv"],
                    bqkv3_ap=aps["bqkv3"], bqkvr_ap=aps["bqkvr"])
        return _mkprog(emit, ext_in, ext_out,
                       ["id128", "ones1x128_bf", "cs_q", "sn_q",
                        "cs_k", "sn_k", "rsign"])


def build_p4(L):
    if L == 0:
        ext_in = ["ctxT0", "g0", "x", "wo"]
        ext_out = ["xmid"]

        def emit(c, aps):
            emit_p4(c, 0, aps["ctxT0"], aps["g0"], aps["x"], aps["xmid"],
                    aps["wo"])
        return _mkprog(emit, ext_in, ext_out, ["fbank", "ones1x128_bf"])
    else:
        ext_in = ["ctxT1", "g1", "xmid", "wo1", "bo1", "gam", "bet", "xnb1"]
        ext_out = ["y"]

        def emit(c, aps):
            emit_p4(c, 1, aps["ctxT1"], aps["g1"], aps["xmid"], aps["y"],
                    aps["wo1"], bo_ap=aps["bo1"], gam_ap=aps["gam"],
                    bet_ap=aps["bet"], xnb_ap=aps["xnb1"])
        return _mkprog(emit, ext_in, ext_out, ["fbank", "ones1x128_bf"])



_PROG_CACHE = {}


def _prog(key, builder):
    if key not in _PROG_CACHE:
        _PROG_CACHE[key] = builder()
    return _PROG_CACHE[key]


def kernel_unfused(**inputs):
    w = prep_weights(inputs)
    x_full = np.ascontiguousarray(
        np.asarray(inputs["L0_attn_hidden_states"], np.float32)
        .reshape(N_CORES, TOK, D))
    cores = list(range(N_CORES))

    def launch(nc, per_core):
        res = run_bass_kernel_spmd(nc, per_core, core_ids=cores)
        return res.results

    sh = [dict() for _ in cores]

    r = launch(_prog("p1_0", lambda: build_p1(0)),
               [{"x": x_full[i], "gateT": w["gateT"]} for i in cores])
    for i in cores:
        sh[i].update(r[i])
    r = launch(_prog("p0", build_p0), [{"relb": w["relb"]} for i in cores])
    for i in cores:
        sh[i].update(r[i])
    r = launch(_prog("p2_0", lambda: build_p2(0)),
               [{"xnT0": sh[i]["xnT0"], "cw0": sh[i]["cw0"], "w1T": w["w1T"],
                 "w3T": w["w3T"], "w2T": w["w2T"]} for i in cores])
    for i in cores:
        sh[i].update(r[i])
    r = launch(_prog("p3_0", lambda: build_p3(0)),
               [{"xnT0": sh[i]["xnT0"], "wq": w["wq"], "wk": w["wk"],
                 "wv": w["wv"], "gbias": sh[i]["gbias"]} for i in cores])
    for i in cores:
        sh[i].update(r[i])
    r = launch(_prog("p4_0", lambda: build_p4(0)),
               [{"ctxT0": sh[i]["ctxT0"], "g0": sh[i]["g0"], "x": x_full[i],
                 "wo": w["wo"]} for i in cores])
    for i in cores:
        sh[i].update(r[i])

    r = launch(_prog("p1_1", lambda: build_p1(1)),
               [{"x": sh[i]["xmid"], "routT": w["routT"]} for i in cores])
    for i in cores:
        sh[i].update(r[i])
    r = launch(_prog("p2_1", lambda: build_p2(1)),
               [{"xnT1": sh[i]["xnT1"], "cw1": sh[i]["cw1"], "fw1": w["fw1"],
                 "fw2": w["fw2"]} for i in cores])
    for i in cores:
        sh[i].update(r[i])
    r = launch(_prog("p3_1", lambda: build_p3(1)),
               [{"xnT1": sh[i]["xnT1"], "wqkv": w["wqkv"],
                 "bqkv3": w["bqkv3"], "bqkvr": w["bqkvr"]} for i in cores])
    for i in cores:
        sh[i].update(r[i])
    r = launch(_prog("p4_1", lambda: build_p4(1)),
               [{"ctxT1": sh[i]["ctxT1"], "g1": sh[i]["g1"],
                 "xmid": sh[i]["xmid"], "wo1": w["wo1"], "bo1": w["bo1"],
                 "gam": w["gam"], "bet": w["bet"], "xnb1": sh[i]["xnb1"]}
                for i in cores])

    y = np.concatenate([r[i]["y"].reshape(SEQ_PER_CORE, SEQ, D)
                        for i in cores], axis=0)
    return np.ascontiguousarray(y.astype(np.float32))



ALL_CONSTS = ["id128", "anti128", "ones1x128_bf", "fbank"]


def build_fused():
    nc = bass.Bass("TRN2", target_bir_lowering=False)
    aps = {}
    aps["x"] = nc.dram_tensor("x", [TOK, D], F32, kind="ExternalInput").ap()
    for nm, (sh, dt) in WEIGHT_SPECS.items():
        aps[nm] = nc.dram_tensor(nm, sh, dt, kind="ExternalInput").ap()
    aps["y"] = nc.dram_tensor("y", [TOK, D], F32, kind="ExternalOutput").ap()
    for nm, (sh, dt) in SCRATCH.items():
        aps[nm] = nc.dram_tensor(nm, sh, dt, kind="Internal").ap()

    consts = host_consts()
    with contextlib.ExitStack() as stack:
        block = stack.enter_context(nc.Block())
        c = Ctx(nc, stack, block)
        load_consts(c, consts, ALL_CONSTS)

        emit_p1(c, 0, aps["x"], aps["xnT0"], aps["cw0"], None, aps["gateT"])
        end_phase(c)
        emit_gbias(c, aps["relb"], aps["gbias"])
        end_phase(c)
        emit_p2(c, 0, aps["xnT0"], aps["cw0"], aps["g0"],
                lambda e, g: aps["w1T"][e, :, g * 512:(g + 1) * 512],
                lambda e, g: aps["w3T"][e, :, g * 512:(g + 1) * 512],
                lambda e, g: aps["w2T"][e, g * 512:(g + 1) * 512, :],
                NIC=32)
        end_phase(c)
        emit_p3(c, 0, aps["xnT0"], aps["ctxT0"], wq_ap=aps["wq"],
                wk_ap=aps["wk"], wv_ap=aps["wv"], gbias_ap=aps["gbias"])
        end_phase(c)
        emit_p4(c, 0, aps["ctxT0"], aps["g0"], aps["x"], aps["xmid"],
                aps["wo"])
        end_phase(c)
        emit_p1(c, 1, aps["xmid"], aps["xnT1"], aps["cw1"], aps["xnb1"],
                aps["routT"])
        end_phase(c)
        emit_p2(c, 1, aps["xnT1"], aps["cw1"], aps["g1"],
                lambda e, g: aps["fw1"][:, e * 1024 + g * 512:
                                        e * 1024 + (g + 1) * 512],
                None,
                lambda e, g: aps["fw2"][e * 1024 + g * 512:
                                        e * 1024 + (g + 1) * 512, :],
                NIC=8)
        end_phase(c)
        emit_p3(c, 1, aps["xnT1"], aps["ctxT1"], wqkv_ap=aps["wqkv"],
                bqkv3_ap=aps["bqkv3"], bqkvr_ap=aps["bqkvr"])
        end_phase(c)
        emit_p4(c, 1, aps["ctxT1"], aps["g1"], aps["xmid"], aps["y"],
                aps["wo1"], bo_ap=aps["bo1"], gam_ap=aps["gam"],
                bet_ap=aps["bet"], xnb_ap=aps["xnb1"])

        cse = c.cs["_sem"]

        def fin(sync):
            sync.wait_ge(cse.h, cse.n)
        c.block.sync(fin)
    return nc


def kernel(**inputs):
    w = prep_weights(inputs)
    x_full = np.ascontiguousarray(
        np.asarray(inputs["L0_attn_hidden_states"], np.float32)
        .reshape(N_CORES, TOK, D))
    nc = _prog("fused", build_fused)
    in_maps = []
    for i in range(N_CORES):
        m = {"x": x_full[i]}
        m.update(w)
        in_maps.append(m)
    res = run_bass_kernel_spmd(nc, in_maps, core_ids=list(range(N_CORES)))
    y = np.concatenate([res.results[i]["y"].reshape(SEQ_PER_CORE, SEQ, D)
                        for i in range(N_CORES)], axis=0)
    return np.ascontiguousarray(y.astype(np.float32))


# version 07 (final)
# speedup vs baseline: 1.1124x; 1.1124x over previous
import contextlib
import math

import numpy as np
import ml_dtypes

import concourse.bass as bass
import concourse.mybir as mybir
from concourse.ap import AP
from concourse.bass_utils import run_bass_kernel_spmd

F32 = mybir.dt.float32
BF16 = mybir.dt.bfloat16
AF = mybir.ActivationFunctionType
ALU = mybir.AluOpType
AX = mybir.AxisListType
BF = ml_dtypes.bfloat16

N_CORES = 8
B_FULL, SEQ, D = 16, 2048, 1024
SEQ_PER_CORE = B_FULL // N_CORES
TOK = SEQ_PER_CORE * SEQ
NT = TOK // 128
H, DKV = 16, 64
E = 8
NEG_BIG = -1.0e30
EPS_LN = 1e-5



def _rope_perm():
    p = []
    for h in range(H):
        p.extend(h * DKV + np.arange(0, DKV, 2))
        p.extend(h * DKV + np.arange(1, DKV, 2))
    return np.asarray(p)


def _t5_bucket(delta):
    num_buckets = 16
    ret = (delta > 0).astype(np.int64) * num_buckets
    rp = np.abs(delta)
    max_exact = num_buckets // 2
    is_small = rp < max_exact
    rp_safe = np.maximum(rp, 1)
    rp_large = max_exact + (
        np.log(rp_safe.astype(np.float32) / max_exact)
        / math.log(128 / max_exact) * (num_buckets - max_exact)
    ).astype(np.int64)
    rp_large = np.minimum(rp_large, num_buckets - 1)
    return ret + np.where(is_small, rp, rp_large)


def prep_weights(inp):
    w = {}
    w["wq"] = np.ascontiguousarray(inp["L0_attn_wq"], np.float32)
    w["wk"] = np.ascontiguousarray(inp["L0_attn_wk"], np.float32)
    w["wv"] = np.ascontiguousarray(inp["L0_attn_wv"], np.float32)
    w["wo"] = np.ascontiguousarray(inp["L0_attn_wo"], np.float32)
    w["relb"] = np.ascontiguousarray(inp["L0_attn_rel_bias"], np.float32)
    w["gateT"] = np.ascontiguousarray(np.asarray(inp["L0_ffn_gate_w"]).T, np.float32)
    w["w1T"] = np.ascontiguousarray(np.asarray(inp["L0_ffn_w1"]).transpose(0, 2, 1), np.float32)
    w["w3T"] = np.ascontiguousarray(np.asarray(inp["L0_ffn_w3"]).transpose(0, 2, 1), np.float32)
    w["w2T"] = np.ascontiguousarray(np.asarray(inp["L0_ffn_w2"]).transpose(0, 2, 1), np.float32)

    perm = _rope_perm()
    wqkv = np.asarray(inp["L1_attn_Wqkv"], dtype=np.float32)
    wqkv_p = wqkv.copy()
    wqkv_p[:, 0:1024] = wqkv[:, 0:1024][:, perm]
    wqkv_p[:, 1024:2048] = wqkv[:, 1024:2048][:, perm]
    w["wqkv"] = np.ascontiguousarray(wqkv_p)
    bqkv = np.asarray(inp["L1_attn_bqkv"], dtype=np.float32)
    bqkv_p = bqkv.copy()
    bqkv_p[0:1024] = bqkv[0:1024][perm]
    bqkv_p[1024:2048] = bqkv[1024:2048][perm]
    bq3 = bqkv_p.reshape(3, 16, 64).transpose(2, 1, 0)
    w["bqkv3"] = np.ascontiguousarray(bq3.reshape(64, 48))
    w["bqkvr"] = np.ascontiguousarray(
        bqkv_p.reshape(3, 16, 64).transpose(1, 0, 2).reshape(48, 64))
    w["wo1"] = np.ascontiguousarray(inp["L1_attn_Wo"], np.float32)
    w["bo1"] = np.ascontiguousarray(np.asarray(inp["L1_attn_bo"]).reshape(1, D), np.float32)
    w["gam"] = np.ascontiguousarray(np.asarray(inp["L1_attn_gamma"]).reshape(1, D), np.float32)
    w["bet"] = np.ascontiguousarray(np.asarray(inp["L1_attn_beta"]).reshape(1, D), np.float32)
    w["routT"] = np.ascontiguousarray(np.asarray(inp["L1_ffn_router_w"]).T, np.float32)
    w["fw1"] = np.ascontiguousarray(inp["L1_ffn_w1"], np.float32)
    w["fw2"] = np.ascontiguousarray(inp["L1_ffn_w2"], np.float32)
    return w


def host_consts():
    c = {}
    c["id128"] = np.eye(128, dtype=BF)
    c["anti128"] = np.eye(128, dtype=BF)[::-1].copy()
    c["ones1x128_bf"] = np.ones((1, 128), dtype=BF)
    delta = np.arange(4095) - 2047
    bk = _t5_bucket(delta)
    oh = np.zeros((32, 4095), dtype=BF)
    oh[bk, np.arange(4095)] = 1.0
    c["onehotT"] = oh
    inv_freq = 1.0 / (10000.0 ** (np.arange(0, DKV, 2, dtype=np.float64) / DKV))
    pos = np.arange(SEQ, dtype=np.float64)
    fr = pos[:, None] * inv_freq[None, :]
    cos = np.cos(fr).T.astype(np.float32)
    sin = np.sin(fr).T.astype(np.float32)
    sc = 1.0 / math.sqrt(DKV)
    c["cs_k"] = np.concatenate([cos, cos], 0).astype(BF)
    c["sn_k"] = np.concatenate([sin, sin], 0).astype(BF)
    c["cs_q"] = (np.concatenate([cos, cos], 0) * sc).astype(BF)
    c["sn_q"] = (np.concatenate([sin, sin], 0) * sc).astype(BF)
    c["rsign"] = np.concatenate(
        [-np.ones((32, 1)), np.ones((32, 1))]).astype(np.float32)
    fb = np.zeros((128, 4), np.float32)
    fb[:, 0] = EPS_LN
    fb[:, 1] = -0.5
    fb[:, 2] = -1.0
    fb[:, 3] = 1.0
    c["fbank"] = fb
    return c



class S:
    __slots__ = ("h", "n")

    def __init__(self, h):
        self.h = h
        self.n = 0

    def inc(self, instr, k=1):
        instr.then_inc(self.h, k)
        self.n += k
        return self.n


class Ctx:
    def __init__(self, nc, stack, block):
        self.nc = nc
        self.stack = stack
        self.block = block
        self.cs = {}
        self._snum = 0
        self._uid = 0
        self.phase_sems = []

    def sem(self, stack=None):
        self._snum += 1
        h = self.nc.alloc_semaphore(f"s{self._snum}")
        self.phase_sems.append(h)
        return S(h)

    def psem(self):
        self._snum += 1
        return S(self.nc.alloc_semaphore(f"sp{self._snum}"))

    def uid(self):
        self._uid += 1
        return self._uid


def end_phase(c):
    emit_barrier(c)
    sems = list(c.phase_sems)
    c.phase_sems = []
    if not hasattr(c, "_bsem2"):
        c._bsem2 = c.psem()
        c._ep = 0
    c._ep += 1
    ep = c._ep

    def f(gp):
        c.nc._clear_and_free_semaphores(c.nc.gpsimd, sems)
        gp.nop().then_inc(c._bsem2.h, 1)
    c.block.gpsimd(f)
    for en in ("sync", "tensor", "vector", "scalar"):
        def g(eng, ep=ep):
            eng.wait_ge(c._bsem2.h, ep)
        getattr(c.block, en)(g)


def CE(eng, sem, instr):
    n = sem.inc(instr)
    eng.wait_ge(sem.h, n)
    return instr


def emit_barrier(c):
    b = c.sem()
    for en in ("sync", "tensor", "vector", "scalar", "gpsimd"):
        def f(eng, b=b):
            eng.nop().then_inc(b.h, 1)
            eng.wait_ge(b.h, 5)
        getattr(c.block, en)(f)
    b.n = 5


def load_consts(c, consts, names, stack=None):
    done = c.cs.get("_sem")
    if done is None:
        done = c.psem()
        c.cs["_sem"] = done
    handles = {}
    aps = {}
    for nm in names:
        arr = consts[nm]
        t = c.nc.inline_tensor(arr, f"c_{nm}")
        dt = BF16 if arr.dtype == BF else F32
        handles[nm] = (stack or c.stack).enter_context(
            c.nc.sbuf_tensor(f"cs_{nm}", list(arr.shape), dt))
        aps[nm] = t.ap()

    def f(sync):
        for nm in names:
            done.inc(sync.dma_start(out=handles[nm][:], in_=aps[nm]), 16)
    c.block.sync(f)
    c.cs.update(handles)
    return handles



def emit_p1(c, L, x_ap, xnT_ap, cw_ap, xnb_ap, gateT_ap):
    nc = c.nc
    u = c.uid()
    with contextlib.ExitStack() as st:
        def sb(nm, sh, dt):
            return st.enter_context(nc.sbuf_tensor(f"p1{u}_{nm}", sh, dt))

        def ps(nm, sh, dt=F32):
            return st.enter_context(nc.psum_tensor(f"p1{u}_{nm}", sh, dt))

        xbuf = [sb(f"x{i}", [128, 1024], F32) for i in range(2)]
        scr = sb("scr", [128, 1024], F32)
        stat2 = [sb(f"stat{i}", [128, 8], F32) for i in range(2)]
        xnb = [sb(f"xnb{i}", [128, 1024], BF16) for i in range(2)]
        stage = sb("stage", [128, 4096], BF16)
        cwst = sb("cwst", [128, 32], F32)
        pu2 = [sb(f"pu{i}", [128, 8], F32) for i in range(2)]
        msk = sb("msk", [128, 8], F32)
        cwr = sb("cwr", [128, 8], F32)
        rst = sb("rst", [128, 4], F32)
        gates = sb("gates", [128, 64], BF16)
        pst = [ps(f"t{i}", [128, 1024], BF16) for i in range(2)]
        psr = [ps(f"r{i}", [128, 8], F32) for i in range(2)]

        ldx = [c.sem(), c.sem()]
        ldw = c.sem()
        s_ss = c.sem()
        s_rstd = c.sem()
        s_pu = c.sem()
        s_var = c.sem()
        s_xn = c.sem()
        s_ev = c.sem()
        s_cw = c.sem()
        s_negm = c.sem()
        s_tp = c.sem()
        s_rmm = c.sem()
        stg = c.sem()
        xnbs = [c.sem(), c.sem()]
        vself = c.sem()
        sself = c.sem()

        NG = NT // 4

        @c.block.gpsimd
        def _(gp):
            src = gateT_ap.rearrange("(c p) e -> p c e", p=128)
            dst = gates.ap().rearrange("p (c e) -> p c e", c=8)
            ldw.inc(gp.dma_start(out=dst, in_=src), 16)

        @c.block.sync
        def _(sync):
            for t in range(NT):
                if t >= 2:
                    sync.wait_ge(s_xn.h, t - 1)
                    sync.wait_ge(s_ss.h, t - 1)
                ldx[t % 2].inc(sync.dma_start(
                    out=xbuf[t % 2][:], in_=x_ap[t * 128:(t + 1) * 128, :]), 16)
                if t % 4 == 3:
                    g = t // 4
                    sync.wait_ge(s_ev.h, t + 1)
                    sync.wait_ge(s_cw.h, t + 1)
                    for ch in range(8):
                        stg.inc(sync.dma_start(
                            out=xnT_ap[ch, :, g * 512:(g + 1) * 512],
                            in_=stage[:, ch * 512:(ch + 1) * 512]), 16)
                    cwv = cw_ap[g * 4:(g + 1) * 4, :, :].rearrange(
                        "t p e -> p t e")
                    stg.inc(sync.dma_start(
                        out=cwv,
                        in_=cwst.ap().rearrange("p (t e) -> p t e", t=4)), 16)
                    sync.wait_ge(stg.h, stg.n)
            if L == 1:
                sync.wait_ge(xnbs[0].h, 16 * (NT // 2))
                sync.wait_ge(xnbs[1].h, 16 * (NT // 2))

        @c.block.scalar
        def _(s):
            cse = c.cs["_sem"]
            s.wait_ge(cse.h, cse.n)
            for t in range(NT):
                stat = stat2[t % 2]
                pu = pu2[t % 2]
                s.wait_ge(ldx[t % 2].h, 16 * (t // 2 + 1))
                if t >= 2:
                    s.wait_ge(s_xn.h, t - 1)
                CE(s, sself, s.activation(scr[:], xbuf[t % 2][:], AF.Square,
                                          accum_out=stat[:, 1:2]))
                s_ss.inc(s.nop())
                s.wait_ge(s_var.h, t + 1)
                CE(s, sself, s.activation(stat[:, 4:5], stat[:, 3:4], AF.Ln,
                                          bias=c.cs["fbank"][:, 0:1]))
                CE(s, sself, s.activation(stat[:, 5:6], stat[:, 4:5], AF.Exp,
                                          scale=c.cs["fbank"][:, 1:2]))
                s_rstd.inc(s.nop())
                s.wait_ge(s_rmm.h, t + 1)
                if t >= 2:
                    s.wait_ge(s_cw.h, t - 1)
                if L == 0:
                    s.wait_ge(s_negm.h, t + 1)
                    CE(s, sself, s.activation(pu[:], psr[t % 2][:], AF.Exp,
                                              bias=stat[:, 7:8]))
                else:
                    CE(s, sself, s.activation(pu[:], psr[t % 2][:], AF.Exp,
                                              scale=c.cs["fbank"][:, 2:3]))
                s_pu.inc(s.nop())

        @c.block.vector
        def _(v):
            for t in range(NT):
                stat = stat2[t % 2]
                pu = pu2[t % 2]
                v.wait_ge(ldx[t % 2].h, 16 * (t // 2 + 1))
                CE(v, vself, v.reduce_sum(stat[:, 0:1], xbuf[t % 2][:], AX.X))
                CE(v, vself, v.tensor_scalar(stat[:, 2:3], stat[:, 0:1],
                                             1.0 / D, None, ALU.mult))
                CE(v, vself, v.tensor_tensor(stat[:, 6:7], stat[:, 2:3],
                                             stat[:, 2:3], ALU.mult))
                v.wait_ge(s_ss.h, t + 1)
                CE(v, vself, v.scalar_tensor_tensor(
                    stat[:, 3:4], stat[:, 1:2], 1.0 / D, stat[:, 6:7],
                    ALU.mult, ALU.subtract))
                s_var.inc(v.engine_nop())
                v.wait_ge(s_rstd.h, t + 1)
                if L == 1 and t >= 2:
                    v.wait_ge(xnbs[t % 2].h, 16 * (t // 2))
                CE(v, vself, v.tensor_scalar(
                    xnb[t % 2][:], xbuf[t % 2][:], stat[:, 2:3], stat[:, 5:6],
                    ALU.subtract, ALU.mult))
                s_xn.inc(v.engine_nop())
                v.wait_ge(s_tp.h, t + 1)
                if t >= 4:
                    v.wait_ge(stg.h, 144 * (t // 4))
                pv = pst[t % 2].ap().rearrange("p (c f) -> p c f", c=8)
                sv = stage.ap().rearrange("p (c g f) -> p c g f", c=8, g=4)
                CE(v, vself, v.tensor_copy(sv[:, :, t % 4, :], pv))
                s_ev.inc(v.engine_nop())
                if L == 0:
                    v.wait_ge(s_rmm.h, t + 1)
                    if t >= 2:
                        v.wait_ge(s_pu.h, t - 1)
                    CE(v, vself, v.reduce_max(stat[:, 7:8], psr[t % 2][:],
                                              AX.X, negate=True))
                    s_negm.inc(v.engine_nop())
                v.wait_ge(s_pu.h, t + 1)
                if L == 1:
                    CE(v, vself, v.tensor_scalar(pu[:], pu[:], 1.0, None,
                                                 ALU.add))
                    CE(v, vself, v.reciprocal(pu[:], pu[:]))
                CE(v, vself, v.reduce_max(rst[:, 0:1], pu[:], AX.X))
                CE(v, vself, v.tensor_scalar(msk[:], pu[:], rst[:, 0:1], None,
                                             ALU.is_ge))
                CE(v, vself, v.scalar_tensor_tensor(msk[:], msk[:], NEG_BIG,
                                                    pu[:], ALU.mult, ALU.add))
                CE(v, vself, v.reduce_max(rst[:, 1:2], msk[:], AX.X))
                CE(v, vself, v.scalar_tensor_tensor(cwr[:], pu[:], rst[:, 1:2],
                                                    pu[:], ALU.is_ge,
                                                    ALU.mult))
                CE(v, vself, v.reduce_sum(rst[:, 2:3], cwr[:], AX.X))
                if L == 1:
                    CE(v, vself, v.tensor_scalar(rst[:, 2:3], rst[:, 2:3],
                                                 1e-20, None, ALU.add))
                CE(v, vself, v.reciprocal(rst[:, 3:4], rst[:, 2:3]))
                CE(v, vself, v.tensor_scalar(
                    cwst[:, (t % 4) * 8:(t % 4) * 8 + 8], cwr[:],
                    rst[:, 3:4], None, ALU.mult))
                s_cw.inc(v.engine_nop())

        @c.block.tensor
        def _(tn):
            cse = c.cs["_sem"]
            tn.wait_ge(cse.h, cse.n)
            tn.wait_ge(ldw.h, 16)
            for t in range(NT):
                tn.wait_ge(s_xn.h, t + 1)
                if t >= 2:
                    tn.wait_ge(s_ev.h, t - 1)
                for ch in range(8):
                    ins = tn.transpose(
                        pst[t % 2][:, ch * 128:(ch + 1) * 128],
                        xnb[t % 2][:, ch * 128:(ch + 1) * 128],
                        c.cs["id128"][:])
                s_tp.inc(ins)
                tn.wait_ge(s_ev.h, t + 1)
                if t >= 2:
                    tn.wait_ge(s_pu.h, t - 1)
                    if L == 0:
                        tn.wait_ge(s_negm.h, t - 1)
                for ch in range(8):
                    ins = tn.matmul(
                        psr[t % 2][:],
                        stage[:, ch * 512 + (t % 4) * 128:
                              ch * 512 + (t % 4) * 128 + 128],
                        gates[:, ch * 8:(ch + 1) * 8],
                        start=(ch == 0), stop=(ch == 7))
                s_rmm.inc(ins)

        if L == 1:
            @c.block.gpsimd
            def _(gp):
                for t in range(NT):
                    gp.wait_ge(s_xn.h, t + 1)
                    xnbs[t % 2].inc(gp.dma_start(
                        out=xnb_ap[t * 128:(t + 1) * 128, :],
                        in_=xnb[t % 2][:]), 16)




def emit_p2(c, L, xnT_ap, cw_ap, g_ap, w1_slice, w3_slice, w2_slice, NIC):
    nc = c.nc
    u = c.uid()
    NB = 4
    has_w3 = L == 0
    act = AF.Silu if L == 0 else AF.Relu
    with contextlib.ExitStack() as st:
        def sb(nm, sh, dt):
            return st.enter_context(nc.sbuf_tensor(f"p2{L}{u}_{nm}", sh, dt))

        xnb = [sb(f"xn{i}", [128, 8 * 1024], BF16) for i in range(2)]
        cwb = [sb(f"cw{i}", [128, 64], F32) for i in range(2)]
        w1c = [sb(f"w1{i}", [128, 8 * 512], BF16) for i in range(2)]
        w3c = [sb(f"w3{i}", [128, 8 * 512], BF16) for i in range(2)] if has_w3 else None
        w2c = [sb(f"w2{i}", [128, 4 * 1024], BF16) for i in range(2)]
        s1b = [sb(f"s1{i}", [128, 1024], BF16) for i in range(2)]
        hT = sb("hT", [128, NIC * 1024], BF16)
        gacc = sb("gacc", [128, 8 * 1024], F32)
        gout = sb("gout", [128, 8 * 1024], BF16)

        ldxn = [c.sem(), c.sem()]
        ldcw = [c.sem(), c.sem()]
        ldw1 = [c.sem(), c.sem()]
        ldw3 = [c.sem(), c.sem()]
        ldw2 = [c.sem(), c.sem()]
        s_x1 = c.sem()
        s_s1 = c.sem()
        s_h = c.sem()
        s_mm2 = c.sem()
        s_ga = c.sem()
        s_gout = c.sem()
        stg_g = c.sem()
        vself = c.sem()
        sself = c.sem()

        def BE(blk, e):
            return blk * 8 + e

        ps_o = [st.enter_context(
            nc.psum_tensor(f"p2{L}{u}_po{i}", [128, 1024], F32))
            for i in range(4)]
        ps_x1 = [ps_o[0], ps_o[1]]
        ps_x3 = [ps_o[2], ps_o[3]]

        NG4 = NIC // 4

        @c.block.gpsimd
        def _(gp):
            for blk in range(NB):
                for e in range(8):
                    be = BE(blk, e)
                    for g4 in range(NG4):
                        p = g4 % 2
                        if be * NG4 + g4 >= 2:
                            prevg = be * NIC + (g4 - 2) * 4 + 4
                            gp.wait_ge(s_x1.h, prevg)
                        src1 = w1_slice(e, g4).rearrange(
                            "(c p) i -> p c i", p=128)
                        ldw1[p].inc(gp.dma_start(
                            out=w1c[p].ap().rearrange("p (c i) -> p c i", c=8),
                            in_=src1), 16)
                        if has_w3:
                            src3 = w3_slice(e, g4).rearrange(
                                "(c p) i -> p c i", p=128)
                            ldw3[p].inc(gp.dma_start(
                                out=w3c[p].ap().rearrange(
                                    "p (c i) -> p c i", c=8),
                                in_=src3), 16)
                    for shalf in range(2):
                        for g4 in range(NG4):
                            p = g4 % 2
                            gidx = (be * 2 + shalf) * NG4 + g4
                            if gidx >= 2:
                                prevg = (be * 2 + shalf) * NIC + (g4 - 2) * 4 + 4
                                gp.wait_ge(s_mm2.h, prevg)
                            src2 = w2_slice(e, g4).rearrange(
                                "(r p) d -> p r d", p=128)
                            ldw2[p].inc(gp.dma_start(
                                out=w2c[p].ap().rearrange(
                                    "p (r d) -> p r d", r=4),
                                in_=src2), 16)

        @c.block.sync
        def _(sync):
            for blk in range(NB):
                if blk >= 2:
                    sync.wait_ge(s_x1.h, (blk - 1) * 8 * NIC)
                    sync.wait_ge(s_ga.h, (blk - 1) * 16)
                for ch in range(8):
                    ldxn[blk % 2].inc(sync.dma_start(
                        out=xnb[blk % 2][:, ch * 1024:(ch + 1) * 1024],
                        in_=xnT_ap[ch, :, blk * 1024:(blk + 1) * 1024]), 16)
                cwv = cw_ap[blk * 8:(blk + 1) * 8, :, :].rearrange(
                    "t p e -> p t e")
                ldcw[blk % 2].inc(sync.dma_start(
                    out=cwb[blk % 2].ap().rearrange("p (t e) -> p t e", t=8),
                    in_=cwv), 16)
                if blk >= 1:
                    sync.wait_ge(s_gout.h, blk)
                    gv = g_ap[(blk - 1) * 1024:blk * 1024, :].rearrange(
                        "(t p) d -> p t d", p=128)
                    stg_g.inc(sync.dma_start(
                        out=gv,
                        in_=gout.ap().rearrange("p (t d) -> p t d", t=8)), 16)
                    sync.wait_ge(stg_g.h, stg_g.n)
            sync.wait_ge(s_gout.h, NB)
            gv = g_ap[(NB - 1) * 1024:NB * 1024, :].rearrange(
                "(t p) d -> p t d", p=128)
            stg_g.inc(sync.dma_start(
                out=gv,
                in_=gout.ap().rearrange("p (t d) -> p t d", t=8)), 16)
            sync.wait_ge(stg_g.h, 16 * NB)

        @c.block.tensor
        def _(tn):
            for blk in range(NB):
                tn.wait_ge(ldxn[blk % 2].h, (blk // 2 + 1) * 128)
                for e in range(8):
                    be = BE(blk, e)
                    for ic in range(NIC):
                        p = ic % 2
                        g4 = ic // 4
                        pw = g4 % 2
                        io = ic % 4
                        if io == 0:
                            tn.wait_ge(ldw1[pw].h,
                                       16 * (be * (NG4 // 2) + g4 // 2 + 1))
                            if has_w3:
                                tn.wait_ge(ldw3[pw].h,
                                           16 * (be * (NG4 // 2) + g4 // 2 + 1))
                        if ic < 2 and be >= 1:
                            tn.wait_ge(s_ga.h, be * 2)
                        if be * NIC + ic >= 2:
                            tn.wait_ge(s_h.h, be * NIC + ic - 1)
                        for half in range(2):
                            for kc in range(8):
                                ins = tn.matmul(
                                    ps_x1[p][:, half * 512:(half + 1) * 512],
                                    w1c[pw][:, kc * 512 + io * 128:
                                            kc * 512 + io * 128 + 128],
                                    xnb[blk % 2][:, kc * 1024 + half * 512:
                                                 kc * 1024 + half * 512 + 512],
                                    start=(kc == 0), stop=(kc == 7))
                        if has_w3:
                            for half in range(2):
                                for kc in range(8):
                                    ins = tn.matmul(
                                        ps_x3[p][:, half * 512:(half + 1) * 512],
                                        w3c[pw][:, kc * 512 + io * 128:
                                                kc * 512 + io * 128 + 128],
                                        xnb[blk % 2][:, kc * 1024 + half * 512:
                                                     kc * 1024 + half * 512 + 512],
                                        start=(kc == 0), stop=(kc == 7))
                        s_x1.inc(ins)
                    for shalf in range(2):
                        if be * 2 + shalf >= 1:
                            tn.wait_ge(s_ga.h, be * 2 + shalf)
                        for ic in range(NIC):
                            g4 = ic // 4
                            pw = g4 % 2
                            io = ic % 4
                            if io == 0:
                                g2 = (be * 2 + shalf) * NG4 + g4
                                tn.wait_ge(ldw2[pw].h, 16 * (g2 // 2 + 1))
                            if shalf == 0 and ic == 0:
                                tn.wait_ge(s_h.h, be * NIC + NIC)
                            for sub in range(4):
                                gsub = shalf * 4 + sub
                                for dch in range(2):
                                    ins = tn.matmul(
                                        ps_o[sub][:, dch * 512:(dch + 1) * 512],
                                        hT[:, ic * 1024 + gsub * 128:
                                           ic * 1024 + gsub * 128 + 128],
                                        w2c[pw][:, io * 1024 + dch * 512:
                                                io * 1024 + dch * 512 + 512],
                                        start=(ic == 0), stop=(ic == NIC - 1))
                            s_mm2.inc(ins)

        @c.block.scalar
        def _(sc):
            for blk in range(NB):
                for e in range(8):
                    be = BE(blk, e)
                    for ic in range(NIC):
                        p = ic % 2
                        sc.wait_ge(s_x1.h, be * NIC + ic + 1)
                        if be * NIC + ic >= 2:
                            sc.wait_ge(s_h.h, be * NIC + ic - 1)
                        CE(sc, sself, sc.activation(s1b[p][:], ps_x1[p][:],
                                                    act))
                        s_s1.inc(sc.nop())
                sc.wait_ge(s_ga.h, (blk + 1) * 16)
                if blk >= 1:
                    sc.wait_ge(stg_g.h, 16 * blk)
                for sub in range(8):
                    CE(sc, sself, sc.activation(
                        gout[:, sub * 1024:(sub + 1) * 1024],
                        gacc[:, sub * 1024:(sub + 1) * 1024], AF.Silu))
                s_gout.inc(sc.nop())

        @c.block.vector
        def _(v):
            for blk in range(NB):
                v.wait_ge(ldcw[blk % 2].h, 16 * (blk // 2 + 1))
                for e in range(8):
                    be = BE(blk, e)
                    for ic in range(NIC):
                        p = ic % 2
                        v.wait_ge(s_s1.h, be * NIC + ic + 1)
                        if e > 0 and ic == 0:
                            v.wait_ge(s_mm2.h, be * 2 * NIC)
                        if has_w3:
                            CE(v, vself, v.tensor_tensor(
                                hT[:, ic * 1024:(ic + 1) * 1024],
                                s1b[p][:], ps_x3[p][:], ALU.mult))
                        else:
                            CE(v, vself, v.tensor_tensor(
                                hT[:, ic * 1024:(ic + 1) * 1024],
                                s1b[p][:], s1b[p][:], ALU.mult))
                        s_h.inc(v.engine_nop())
                    for shalf in range(2):
                        v.wait_ge(s_mm2.h, (be * 2 + shalf + 1) * NIC)
                        if blk >= 1 and e == 0 and shalf == 0:
                            v.wait_ge(s_gout.h, blk)
                        for sub in range(4):
                            gsub = shalf * 4 + sub
                            cwsl = cwb[blk % 2][:, gsub * 8 + e:gsub * 8 + e + 1]
                            dst = gacc[:, gsub * 1024:(gsub + 1) * 1024]
                            if e == 0:
                                CE(v, vself, v.tensor_scalar(
                                    dst, ps_o[sub][:], cwsl, None, ALU.mult))
                            else:
                                CE(v, vself, v.scalar_tensor_tensor(
                                    dst, ps_o[sub][:], cwsl, dst,
                                    ALU.mult, ALU.add))
                        s_ga.inc(v.engine_nop())




def emit_gbias(c, relb_ap, gbias_ap):
    nc = c.nc
    u = c.uid()
    with contextlib.ExitStack() as st:
        if "onehotT" not in c.cs:
            load_consts(c, host_consts(), ["onehotT"], stack=st)
        relb = st.enter_context(nc.sbuf_tensor(f"p0{u}_relb", [32, 16], BF16))
        gsb = st.enter_context(nc.sbuf_tensor(f"p0{u}_g", [16, 4095], BF16))
        psg = st.enter_context(nc.psum_tensor(f"p0{u}_ps", [16, 512], F32))
        ldr = c.sem()
        s_mm = c.sem()
        s_ev = c.sem()
        stg = c.sem()

        @c.block.gpsimd
        def _(gp):
            ldr.inc(gp.dma_start(out=relb[:], in_=relb_ap), 16)

        @c.block.tensor
        def _(tn):
            cse = c.cs["_sem"]
            tn.wait_ge(cse.h, cse.n)
            tn.wait_ge(ldr.h, 16)
            for j in range(8):
                w = min(512, 4095 - j * 512)
                if j >= 1:
                    tn.wait_ge(s_ev.h, j)
                s_mm.inc(tn.matmul(psg[:, 0:w], relb[:],
                                   c.cs["onehotT"][:, j * 512:j * 512 + w],
                                   start=True, stop=True))

        @c.block.vector
        def _(v):
            for j in range(8):
                w = min(512, 4095 - j * 512)
                v.wait_ge(s_mm.h, j + 1)
                s_ev.inc(v.tensor_copy(gsb[:, j * 512:j * 512 + w],
                                       psg[:, 0:w]))

        @c.block.sync
        def _(sync):
            sync.wait_ge(s_ev.h, 8)
            stg.inc(sync.dma_start(out=gbias_ap, in_=gsb[:]), 16)
            sync.wait_ge(stg.h, 16)




def emit_p3(c, L, xnT_ap, ctxT_ap, wq_ap=None, wk_ap=None, wv_ap=None,
            gbias_ap=None, wqkv_ap=None, bqkv3_ap=None, bqkvr_ap=None):
    nc = c.nc
    u = c.uid()
    pre = f"p3{L}{u}"
    with contextlib.ExitStack() as st:
        if L == 1 and "cs_q" not in c.cs:
            load_consts(c, host_consts(),
                        ["cs_q", "sn_q", "cs_k", "sn_k", "rsign"], stack=st)
        def sb(nm, sh, dt):
            return st.enter_context(nc.sbuf_tensor(f"{pre}_{nm}", sh, dt))

        PF = st.enter_context(nc.psum_tensor(f"{pre}_PF", [128, 2048], F32))
        PS0 = st.enter_context(nc.psum_tensor(f"{pre}_PS0", [128, 512], F32))
        PS1 = st.enter_context(nc.psum_tensor(f"{pre}_PS1", [128, 512], F32))
        PSC = st.enter_context(nc.psum_tensor(f"{pre}_PSC", [128, 512], F32))
        PT0 = st.enter_context(nc.psum_tensor(f"{pre}_PT0", [128, 512], F32))
        xnq = sb("xnq", [128, 8 * 4096], BF16)
        wqh = sb("wqh", [128, 512], BF16)
        wkh = sb("wkh", [128, 512], BF16)
        wvh = sb("wvh", [128, 512], BF16)
        qT2 = [sb(f"qT{i}", [64, 2048], BF16) for i in range(2)]
        kT2 = [sb(f"kT{i}", [64, 2048], BF16) for i in range(2)]
        vaug = sb("vaug", [128, 16 * 65], BF16)
        prb = [sb(f"prb{i}", [128, 512], BF16) for i in range(2)]
        czb = sb("czb", [65, 2048], BF16)
        sq2 = sb("sq2", [64, 4096], BF16)
        nfold = sb("nfold", [1, 16], F32)
        ones64 = sb("ones64", [64, 1], BF16)
        fold = sb("fold", [128, 8], F32)
        ftile = sb("ftile", [32, 32], F32)
        ftileT = sb("ftileT", [32, 32], F32)
        negMbf = sb("negMbf", [1, 1], BF16)
        negM = sb("negM", [128, 1], F32)
        ctok = [sb(f"ctok{i}", [128, 64], BF16) for i in range(2)]
        cstg = [sb(f"cstg{i}", [128, 2048], BF16) for i in range(2)]
        if L == 0:
            lqsb = sb("lq", [128, 31 * 128], BF16)
            barr2 = [sb(f"barr{i}", [128, 31 * 128], BF16) for i in range(2)]
            gb_all = sb("gb", [16, 4095], BF16)
        else:
            qb = sb("qb", [64, 2048], BF16)
            kb = sb("kb", [64, 2048], BF16)
            uu = sb("uu", [64, 2048], BF16)
            usw = sb("usw", [64, 2048], BF16)
            qcs = sb("qcs", [64, 2048], BF16)
            bq3 = sb("bq3", [64, 48], F32)
            bvr = sb("bvr", [1, 64], BF16)

        ldxn = c.sem()
        ldh = c.sem()
        ldb3 = c.sem()
        s_bbm = c.sem()
        s_bb = c.sem()
        s_qk = c.sem()
        s_qbk = c.sem()
        s_qke = c.sem()
        s_vt = c.sem()
        s_vev = c.sem()
        s_sq = c.sem()
        s_nmm = c.sem()
        s_mx = c.sem()
        ldgb = c.sem()
        s_fold = c.sem()
        s_bc = c.sem()
        s_nm = c.sem()
        s_qkmm = c.sem()
        s_exp = c.sem()
        s_avk = c.sem()
        s_cze = c.sem()
        s_tr1 = c.sem()
        s_rz = c.sem()
        s_tr2 = c.sem()
        s_cst = c.sem()
        stgc = [c.sem(), c.sem()]
        vself = c.sem()
        sself = c.sem()

        NH_DMA = 4

        @c.block.sync
        def _(sync):
            for ch in range(8):
                ldxn.inc(sync.dma_start(
                    out=xnq[:, ch * 4096:(ch + 1) * 4096],
                    in_=xnT_ap[ch, :, :]), 16)
            for h in range(1, 16, 2):
                for sq in range(2):
                    sync.wait_ge(s_cst.h, 16 * (h * 2 + sq + 1))
                    stgc[sq].inc(sync.dma_start(
                        out=ctxT_ap[h // 2, :, sq * 2048:(sq + 1) * 2048],
                        in_=cstg[sq][:]), 16)
                    sync.wait_ge(stgc[sq].h, stgc[sq].n)

        @c.block.gpsimd
        def _(gp):
            if L == 0:
                ldgb.inc(gp.dma_start(out=gb_all[:], in_=gbias_ap), 16)
            if L == 1:
                ldb3.inc(gp.dma_start(out=bq3[:], in_=bqkv3_ap), 16)
            for h in range(16):
                if h >= 1:
                    gp.wait_ge(s_vt.h, (h * 2) * 16)
                    if L == 0:
                        gp.wait_ge(s_bb.h, 31 * h)
                if L == 0:
                    srcs = [wq_ap[:, h * 64:(h + 1) * 64],
                            wk_ap[:, h * 64:(h + 1) * 64],
                            wv_ap[:, h * 64:(h + 1) * 64]]
                else:
                    srcs = [wqkv_ap[:, h * 64:(h + 1) * 64],
                            wqkv_ap[:, 1024 + h * 64:1024 + (h + 1) * 64],
                            wqkv_ap[:, 2048 + h * 64:2048 + (h + 1) * 64]]
                for dst, src in zip((wqh, wkh, wvh), srcs):
                    ldh.inc(gp.dma_start(
                        out=dst.ap().rearrange("p (c o) -> p c o", c=8),
                        in_=src.rearrange("(c p) o -> p c o", p=128)), 16)
                if L == 0:
                    src = AP(gbias_ap.tensor, h * 4095,
                             [[1, 128], [128, 31], [1, 128]])
                    ldh.inc(gp.dma_start(
                        out=lqsb.ap().rearrange("p (i k) -> p i k", i=31),
                        in_=src), 16)
                else:
                    ldh.inc(gp.dma_start(
                        out=bvr[:], in_=bqkvr_ap[h * 3 + 2:h * 3 + 3, :]), 16)

        @c.block.tensor
        def _(tn):
            cse = c.cs["_sem"]
            tn.wait_ge(cse.h, cse.n)
            tn.wait_ge(ldxn.h, 128)
            for h in range(16):
                for sq in range(2):
                    hs = h * 2 + sq
                    tn.wait_ge(ldh.h, 16 * NH_DMA * (h + 1))
                    if hs >= 1:
                        tn.wait_ge(s_vev.h, 16 * hs)
                        tn.wait_ge(s_mx.h, 2 * hs)
                    do_build = (L == 0 and sq == 0)
                    if do_build:
                        bslots = [PS0, PS1, PSC, PT0]
                        if hs >= 1:
                            tn.wait_ge(s_cst.h, 16 * hs)
                            tn.wait_ge(s_exp.h, 64 * hs)
                            tn.wait_ge(s_cze.h, 4 * hs)
                            tn.wait_ge(s_nm.h, hs)
                    bi = 0
                    for nsl in range(4):
                        for kc in range(8):
                            if do_build and bi < 31:
                                if h * 31 + bi >= 4:
                                    tn.wait_ge(s_bb.h, h * 31 + bi - 3)
                                s_bbm.inc(tn.matmul(
                                    bslots[bi % 4][:, 0:128],
                                    lqsb[:, bi * 128:(bi + 1) * 128],
                                    c.cs["anti128"][:], start=True, stop=True))
                                bi += 1
                            ins = tn.matmul(
                                PF[0:64, nsl * 512:nsl * 512 + 512],
                                wqh[:, kc * 64:(kc + 1) * 64],
                                xnq[:, kc * 4096 + sq * 2048 + nsl * 512:
                                    kc * 4096 + sq * 2048 + nsl * 512 + 512],
                                start=(kc == 0), stop=(kc == 7))
                    for nsl in range(4):
                        for kc in range(8):
                            ins = tn.matmul(
                                PF[64:128, nsl * 512:nsl * 512 + 512],
                                wkh[:, kc * 64:(kc + 1) * 64],
                                xnq[:, kc * 4096 + sq * 2048 + nsl * 512:
                                    kc * 4096 + sq * 2048 + nsl * 512 + 512],
                                start=(kc == 0), stop=(kc == 7),
                                tile_position=(0, 64))
                    s_qk.inc(ins)
                    tn.wait_ge(s_qke.h, hs + 1)
                    for kt in range(16):
                        pv = PF[:, (kt % 2) * 512:(kt % 2) * 512 + 64]
                        if kt >= 2:
                            tn.wait_ge(s_vev.h, hs * 16 + kt - 1)
                        if L == 1:
                            tn.matmul(pv, c.cs["ones1x128_bf"][:], bvr[:],
                                      start=True, stop=False)
                        for kc in range(8):
                            ins = tn.matmul(
                                pv,
                                xnq[:, kc * 4096 + sq * 2048 + kt * 128:
                                    kc * 4096 + sq * 2048 + kt * 128 + 128],
                                wvh[:, kc * 64:(kc + 1) * 64],
                                start=(kc == 0 and L == 0), stop=(kc == 7))
                        s_vt.inc(ins)
                    qT = qT2[hs % 2]
                    kT = kT2[hs % 2]
                    if L == 0:
                        barr = barr2[h % 2]
                    if hs >= 1:
                        tn.wait_ge(s_exp.h, 64 * hs)
                        tn.wait_ge(s_cze.h, 4 * hs)
                        tn.wait_ge(s_cst.h, 16 * hs)
                        tn.wait_ge(s_nm.h, hs)
                    if do_build:
                        tn.wait_ge(s_bb.h, 31 * (h + 1))
                    tn.wait_ge(s_sq.h, hs * 2 + 1)
                    for ch, bank in enumerate((PS0, PS1, PSC, PT0)):
                        ins = tn.matmul(bank[0:1, 0:512], ones64[:],
                                        sq2[:, ch * 512:(ch + 1) * 512],
                                        start=True, stop=True)
                    s_nmm.inc(ins)
                    tn.wait_ge(s_sq.h, hs * 2 + 2)
                    tn.wait_ge(s_mx.h, hs * 2 + 1)
                    for ch, bank in enumerate((PS0, PS1, PSC, PT0)):
                        ins = tn.matmul(bank[0:1, 0:512], ones64[:],
                                        sq2[:, 2048 + ch * 512:
                                            2048 + (ch + 1) * 512],
                                        start=True, stop=True)
                    s_nmm.inc(ins)
                    tn.wait_ge(s_fold.h, hs + 1)
                    if hs >= 1:
                        tn.wait_ge(s_exp.h, 64 * hs)
                        tn.wait_ge(s_nm.h, hs)
                    if L == 0:
                        tn.wait_ge(s_bb.h, 31 * (h + 1))
                    s_bc.inc(tn.matmul(PS0[:, 0:1],
                                       c.cs["ones1x128_bf"][:], negMbf[:],
                                       start=True, stop=True))
                    for qch in range(4):
                        for kt in range(16):
                            gk = hs * 64 + qch * 16 + kt
                            if gk >= 2:
                                tn.wait_ge(s_exp.h, gk - 1)
                            if kt == 0:
                                tn.wait_ge(s_cze.h, hs * 4 + qch)
                                if qch == 0:
                                    tn.wait_ge(s_nm.h, hs + 1)
                                    if L == 0:
                                        tn.wait_ge(s_bb.h, 31 * (h + 1))
                            psd = PS0 if kt % 2 == 0 else PS1
                            ins = tn.matmul(
                                psd[:],
                                kT[:, kt * 128:(kt + 1) * 128],
                                qT[:, qch * 512:(qch + 1) * 512],
                                start=True, stop=(L == 1))
                            if L == 0:
                                j0 = 15 - kt + 4 * qch
                                ins = tn.matmul(
                                    psd[:],
                                    c.cs["id128"][:],
                                    barr[:, 128 * j0:128 * j0 + 512],
                                    start=False, stop=True)
                            s_qkmm.inc(ins)
                            if kt >= 1:
                                tn.wait_ge(s_exp.h, gk)
                                if kt == 1 and qch == 0 and hs >= 1:
                                    tn.wait_ge(s_cst.h, 16 * hs)
                                s_avk.inc(tn.matmul(
                                    PSC[0:65, :],
                                    vaug[:, (kt - 1) * 65:(kt - 1) * 65 + 65],
                                    prb[(kt - 1) % 2][:],
                                    start=(kt == 1), stop=False))
                        gk15 = hs * 64 + qch * 16 + 15
                        tn.wait_ge(s_exp.h, gk15 + 1)
                        s_avk.inc(tn.matmul(
                            PSC[0:65, :],
                            vaug[:, 15 * 65:15 * 65 + 65],
                            prb[1][:], start=False, stop=True))
                    for i in range(16):
                        trv = (PSC if i % 2 == 0 else PT0).ap().bitcast(BF16)
                        if i == 0:
                            tn.wait_ge(s_cze.h, hs * 4 + 4)
                        if hs * 16 + i >= 2:
                            tn.wait_ge(s_cst.h, hs * 16 + i - 1)
                        s_tr1.inc(tn.transpose(
                            trv[:, 0:65],
                            czb[:, i * 128:(i + 1) * 128],
                            c.cs["id128"][0:65, 0:65]))
                        tn.wait_ge(s_rz.h, hs * 16 + i + 1)
                        s_tr2.inc(tn.transpose(
                            trv[0:64, 512:640],
                            ctok[i % 2][:], c.cs["id128"][:]))

        @c.block.scalar
        def _(sc):
            cse = c.cs["_sem"]
            sc.wait_ge(cse.h, cse.n)
            for h in range(16):
                for sq in range(2):
                    hs = h * 2 + sq
                    if L == 0:
                        sc.wait_ge(s_qk.h, hs + 1)
                        if hs >= 2:
                            sc.wait_ge(s_qkmm.h, 64 * (hs - 1))
                        CE(sc, sself, sc.activation(qT2[hs % 2][:],
                                                    PF[0:64, :], AF.Copy))
                        CE(sc, sself, sc.activation(kT2[hs % 2][:],
                                                    PF[64:128, :], AF.Copy))
                        s_qke.inc(sc.nop())
                    else:
                        sc.wait_ge(s_qk.h, hs + 1)
                        sc.wait_ge(ldb3.h, 16)
                        if hs >= 1:
                            sc.wait_ge(s_qke.h, hs)
                        CE(sc, s_qbk, sc.activation(
                            qb[:], PF[0:64, :], AF.Identity,
                            bias=bq3[:, h * 3:h * 3 + 1]))
                        CE(sc, s_qbk, sc.activation(
                            kb[:], PF[64:128, :], AF.Identity,
                            bias=bq3[:, h * 3 + 1:h * 3 + 2]))
                    for qch in range(4):
                        for kt in range(16):
                            gk = hs * 64 + qch * 16 + kt
                            sc.wait_ge(s_qkmm.h, gk + 1)
                            if gk >= 2:
                                sc.wait_ge(s_avk.h, gk - 1)
                            if qch == 0 and kt == 0:
                                sc.wait_ge(s_nm.h, hs + 1)
                            s_exp.inc(sc.activation(
                                prb[kt % 2][:],
                                (PS0 if kt % 2 == 0 else PS1)[:],
                                AF.Exp, bias=negM[:]))
                        sc.wait_ge(s_avk.h, hs * 64 + (qch + 1) * 16)
                        if qch == 0:
                            sc.wait_ge(s_tr1.h, 16 * hs)
                        s_cze.inc(sc.activation(
                            czb[:, qch * 512:(qch + 1) * 512],
                            PSC[0:65, :], AF.Copy))
                    for i in range(16):
                        trv = (PSC if i % 2 == 0 else PT0).ap().bitcast(BF16)
                        sc.wait_ge(s_tr2.h, hs * 16 + i + 1)
                        if i == 0 and h % 2 == 0:
                            sc.wait_ge(stgc[sq].h, 16 * (h // 2))
                        s_cst.inc(sc.activation(
                            cstg[sq][(h % 2) * 64:(h % 2) * 64 + 64,
                                     i * 128:(i + 1) * 128],
                            trv[0:64, 512:640], AF.Copy))

        @c.block.vector
        def _(v):
            CE(v, vself, v.memset(vaug[:], 1.0))
            CE(v, vself, v.memset(ftile[:], 0.0))
            CE(v, vself, v.memset(ones64[:], 1.0))
            if L == 0:
                v.wait_ge(ldgb.h, 16)
                CE(v, vself, v.reduce_max(fold[0:16, 6:7], gb_all[:], AX.X))
                CE(v, vself, v.tensor_copy(ftile[0:16, 0:1],
                                           fold[0:16, 6:7]))
                CE(v, vself, v.transpose(ftileT[0:32, 0:32],
                                         ftile[0:32, 0:32]))
            for h in range(16):
                for sq in range(2):
                    hs = h * 2 + sq
                    if L == 0 and sq == 0:
                        for i in range(31):
                            v.wait_ge(s_bbm.h, h * 31 + i + 1)
                            if h >= 2 and i == 0:
                                v.wait_ge(s_qkmm.h, 64 * (2 * h - 2))
                            CE(v, s_bb, v.tensor_copy(
                                barr2[h % 2][:, (30 - i) * 128:
                                             (30 - i) * 128 + 128],
                                [PS0, PS1, PSC, PT0][i % 4][:, 0:128]))
                    if L == 1:
                        for (tin, tout, csn, snn) in (
                                (qb, qT2[hs % 2], "cs_q", "sn_q"),
                                (kb, kT2[hs % 2], "cs_k", "sn_k")):
                            which = 0 if tin is qb else 1
                            v.wait_ge(s_qbk.h, hs * 2 + which + 1)
                            if hs >= 2 and which == 0:
                                v.wait_ge(s_qkmm.h, 64 * (hs - 1))
                            CE(v, vself, v.tensor_tensor(
                                uu[:], tin[:], c.cs[snn][:], ALU.mult))
                            CE(v, vself, v.tensor_copy(usw[0:32, :],
                                                       uu[32:64, :]))
                            CE(v, vself, v.tensor_copy(usw[32:64, :],
                                                       uu[0:32, :]))
                            CE(v, vself, v.tensor_tensor(
                                qcs[:], tin[:], c.cs[csn][:], ALU.mult))
                            CE(v, vself, v.scalar_tensor_tensor(
                                tout[:], usw[:], c.cs["rsign"][:], qcs[:],
                                ALU.mult, ALU.add))
                        s_qke.inc(v.engine_nop())
                    if L == 0:
                        v.wait_ge(s_qke.h, hs + 1)
                    if hs >= 1:
                        v.wait_ge(s_nmm.h, hs * 2)
                    CE(v, s_sq, v.tensor_tensor(
                        sq2[:, 0:2048], qT2[hs % 2][:], qT2[hs % 2][:],
                        ALU.mult))
                    CE(v, s_sq, v.tensor_tensor(
                        sq2[:, 2048:4096], kT2[hs % 2][:], kT2[hs % 2][:],
                        ALU.mult))
                    for kt in range(16):
                        v.wait_ge(s_vt.h, hs * 16 + kt + 1)
                        if kt == 0 and hs >= 1:
                            v.wait_ge(s_avk.h, 64 * hs)
                        s_vev.inc(v.tensor_copy(
                            vaug[:, kt * 65:kt * 65 + 64],
                            PF[:, (kt % 2) * 512:(kt % 2) * 512 + 64]))
                    for which in range(2):
                        v.wait_ge(s_nmm.h, hs * 2 + which + 1)
                        for ch, bank in enumerate((PS0, PS1, PSC, PT0)):
                            CE(v, vself, v.reduce_max(
                                nfold[0:1, which * 5 + ch:
                                      which * 5 + ch + 1],
                                bank[0:1, 0:512], AX.X))
                        s_mx.inc(v.engine_nop())
                        CE(v, vself, v.reduce_max(
                            nfold[0:1, which * 5 + 4:which * 5 + 5],
                            nfold[0:1, which * 5:which * 5 + 4], AX.X))
                    if hs >= 1:
                        v.wait_ge(s_bc.h, hs)
                    CE(v, vself, v.tensor_tensor(nfold[0:1, 10:11],
                                                 nfold[0:1, 4:5],
                                                 nfold[0:1, 9:10], ALU.add))
                    if L == 0:
                        CE(v, vself, v.scalar_tensor_tensor(
                            negMbf[:], nfold[0:1, 10:11], -0.25,
                            ftileT[0:1, h:h + 1], ALU.mult, ALU.subtract))
                    else:
                        CE(v, vself, v.tensor_scalar(
                            negMbf[:], nfold[0:1, 10:11], -0.5,
                            None, ALU.mult))
                    s_fold.inc(v.engine_nop())
                    v.wait_ge(s_bc.h, hs + 1)
                    CE(v, vself, v.tensor_copy(negM[:], PS0[:, 0:1]))
                    s_nm.inc(v.engine_nop())
                    for i in range(16):
                        trv = (PSC if i % 2 == 0 else PT0).ap().bitcast(BF16)
                        v.wait_ge(s_tr1.h, hs * 16 + i + 1)
                        CE(v, vself, v.reciprocal(fold[:, 4:5],
                                                  trv[:, 64:65]))
                        CE(v, s_rz, v.tensor_scalar(
                            ctok[i % 2][:], trv[:, 0:64],
                            fold[:, 4:5], None, ALU.mult))




def emit_p4(c, L, ctxT_ap, g_ap, x_ap, out_ap, wo_ap,
            bo_ap=None, gam_ap=None, bet_ap=None, xnb_ap=None):
    nc = c.nc
    u = c.uid()
    pre = f"p4{L}{u}"
    with contextlib.ExitStack() as st:
        def sb(nm, sh, dt):
            return st.enter_context(nc.sbuf_tensor(f"{pre}_{nm}", sh, dt))

        ctx_sb = sb("ctx", [128, 8 * 4096], BF16)
        wo_sb = sb("wo", [128, 8 * 1024], BF16)
        xb = [sb(f"x{i}", [128, 1024], F32) for i in range(2)]
        gb = [sb(f"g{i}", [128, 1024], BF16) for i in range(2)]
        t1 = sb("t1", [128, 1024], F32)
        xo = [sb(f"xo{i}", [128, 1024], F32) for i in range(2)]
        ps_s = [st.enter_context(
            nc.psum_tensor(f"{pre}_ps{i}", [128, 1024], F32)) for i in range(2)]
        if L == 1:
            xnb = [sb(f"xnb{i}", [128, 1024], BF16) for i in range(2)]
            bo_sb = sb("bo", [1, 1024], BF16)
            gamF = sb("gamF", [128, 1024], F32)
            betF = sb("betF", [128, 1024], F32)
            gam1 = sb("gam1", [1, 1024], F32)
            bet1 = sb("bet1", [1, 1024], F32)
            ones1 = sb("ones1", [1, 128], F32)
            hsb = sb("h", [128, 1024], F32)
            stat = sb("stat", [128, 8], F32)
            scr = sb("scr", [128, 1024], F32)
            ps_gb = st.enter_context(
                nc.psum_tensor(f"{pre}_psgb", [128, 1024], F32))

        ldc = c.sem()
        ldx2 = [c.sem(), c.sem()]
        ldg2 = [c.sem(), c.sem()]
        ldn2 = [c.sem(), c.sem()]
        s_gb = c.sem()
        s_mm = c.sem()
        s_ss2 = c.sem()
        s_var2 = c.sem()
        s_rstd2 = c.sem()
        s_hln = c.sem()
        s_out = c.sem()
        sto = [c.sem(), c.sem()]
        vself = c.sem()
        sself = c.sem()

        @c.block.gpsimd
        def _(gp):
            for ch in range(8):
                ldc.inc(gp.dma_start(
                    out=ctx_sb[:, ch * 4096:(ch + 1) * 4096],
                    in_=ctxT_ap[ch, :, :]), 16)
            ldc.inc(gp.dma_start(
                out=wo_sb.ap().rearrange("p (cc d) -> p cc d", cc=8),
                in_=wo_ap.rearrange("(cc p) d -> p cc d", p=128)), 16)
            if L == 1:
                ldc.inc(gp.dma_start(out=bo_sb[:], in_=bo_ap), 16)
                ldc.inc(gp.dma_start(out=gam1[:], in_=gam_ap), 16)
                ldc.inc(gp.dma_start(out=bet1[:], in_=bet_ap), 16)

        if L == 1:
            s_ones = c.sem()
            s_gmm = c.sem()

            @c.block.vector
            def _(v):
                CE(v, vself, v.memset(ones1[:], 1.0))
                s_ones.inc(v.engine_nop())
                v.wait_ge(s_gmm.h, 1)
                CE(v, vself, v.tensor_copy(gamF[:], ps_gb[:]))
                s_gb.inc(v.engine_nop())
                v.wait_ge(s_gmm.h, 2)
                CE(v, vself, v.tensor_copy(betF[:], ps_gb[:]))
                s_gb.inc(v.engine_nop())

            @c.block.tensor
            def _(tn):
                tn.wait_ge(ldc.h, 16 * 12)
                tn.wait_ge(s_ones.h, 1)
                for dch in range(2):
                    ins = tn.matmul(ps_gb[:, dch * 512:(dch + 1) * 512],
                                    ones1[:],
                                    gam1[:, dch * 512:(dch + 1) * 512],
                                    start=True, stop=True)
                s_gmm.inc(ins)
                tn.wait_ge(s_gb.h, 1)
                for dch in range(2):
                    ins = tn.matmul(ps_gb[:, dch * 512:(dch + 1) * 512],
                                    ones1[:],
                                    bet1[:, dch * 512:(dch + 1) * 512],
                                    start=True, stop=True)
                s_gmm.inc(ins)

        @c.block.sync
        def _(sync):
            for t in range(NT):
                if t >= 2:
                    sync.wait_ge(s_out.h, t - 1)
                ldx2[t % 2].inc(sync.dma_start(
                    out=xb[t % 2][:], in_=x_ap[t * 128:(t + 1) * 128, :]), 16)
                ldg2[t % 2].inc(sync.dma_start(
                    out=gb[t % 2][:], in_=g_ap[t * 128:(t + 1) * 128, :]), 16)
                if L == 1:
                    ldn2[t % 2].inc(sync.dma_start(
                        out=xnb[t % 2][:],
                        in_=xnb_ap[t * 128:(t + 1) * 128, :]), 16)
                if t >= 1:
                    sync.wait_ge(s_out.h, t)
                    sto[(t - 1) % 2].inc(sync.dma_start(
                        out=out_ap[(t - 1) * 128:t * 128, :],
                        in_=xo[(t - 1) % 2][:]), 16)
                    sync.wait_ge(sto[(t - 1) % 2].h, sto[(t - 1) % 2].n)
            sync.wait_ge(s_out.h, NT)
            sto[(NT - 1) % 2].inc(sync.dma_start(
                out=out_ap[(NT - 1) * 128:NT * 128, :],
                in_=xo[(NT - 1) % 2][:]), 16)
            sync.wait_ge(sto[(NT - 1) % 2].h, sto[(NT - 1) % 2].n)

        @c.block.tensor
        def _(tn):
            cse = c.cs["_sem"]
            tn.wait_ge(cse.h, cse.n)
            tn.wait_ge(ldc.h, ldc.n)
            if L == 1:
                tn.wait_ge(s_gb.h, 2)
            for t in range(NT):
                if t >= 2:
                    tn.wait_ge(s_out.h, t - 1)
                for dch in range(2):
                    pss = ps_s[t % 2][:, dch * 512:(dch + 1) * 512]
                    if L == 1:
                        tn.matmul(pss, c.cs["ones1x128_bf"][:],
                                  bo_sb[:, dch * 512:(dch + 1) * 512],
                                  start=True, stop=False)
                    for ch in range(8):
                        ins = tn.matmul(
                            pss,
                            ctx_sb[:, ch * 4096 + t * 128:
                                   ch * 4096 + t * 128 + 128],
                            wo_sb[:, ch * 1024 + dch * 512:
                                  ch * 1024 + dch * 512 + 512],
                            start=(ch == 0 and L == 0), stop=(ch == 7))
                s_mm.inc(ins)

        if L == 0:
            @c.block.vector
            def _(v):
                for t in range(NT):
                    v.wait_ge(s_mm.h, t + 1)
                    v.wait_ge(ldg2[t % 2].h, 16 * (t // 2 + 1))
                    v.wait_ge(ldx2[t % 2].h, 16 * (t // 2 + 1))
                    if t >= 2:
                        v.wait_ge(sto[t % 2].h, 16 * (t // 2))
                    CE(v, vself, v.tensor_tensor(t1[:], ps_s[t % 2][:],
                                                 gb[t % 2][:], ALU.mult))
                    CE(v, vself, v.tensor_tensor(xo[t % 2][:], xb[t % 2][:],
                                                 t1[:], ALU.add))
                    s_out.inc(v.engine_nop())
        else:
            @c.block.scalar
            def _(sc):
                cse = c.cs["_sem"]
                sc.wait_ge(cse.h, cse.n)
                for t in range(NT):
                    sc.wait_ge(s_hln.h, t + 1)
                    CE(sc, sself, sc.activation(scr[:], hsb[:], AF.Square,
                                                accum_out=stat[:, 1:2]))
                    s_ss2.inc(sc.nop())
                    sc.wait_ge(s_var2.h, t + 1)
                    CE(sc, sself, sc.activation(stat[:, 4:5], stat[:, 3:4],
                                                AF.Ln,
                                                bias=c.cs["fbank"][:, 0:1]))
                    CE(sc, sself, sc.activation(stat[:, 5:6], stat[:, 4:5],
                                                AF.Exp,
                                                scale=c.cs["fbank"][:, 1:2]))
                    s_rstd2.inc(sc.nop())

            @c.block.vector
            def _(v):
                for t in range(NT):
                    v.wait_ge(s_mm.h, t + 1)
                    v.wait_ge(ldn2[t % 2].h, 16 * (t // 2 + 1))
                    CE(v, vself, v.tensor_tensor(hsb[:], xnb[t % 2][:],
                                                 ps_s[t % 2][:], ALU.add))
                    s_hln.inc(v.engine_nop())
                    CE(v, vself, v.reduce_sum(stat[:, 0:1], hsb[:], AX.X))
                    CE(v, vself, v.tensor_scalar(stat[:, 2:3], stat[:, 0:1],
                                                 1.0 / D, None, ALU.mult))
                    CE(v, vself, v.tensor_tensor(stat[:, 6:7], stat[:, 2:3],
                                                 stat[:, 2:3], ALU.mult))
                    v.wait_ge(s_ss2.h, t + 1)
                    CE(v, vself, v.scalar_tensor_tensor(
                        stat[:, 3:4], stat[:, 1:2], 1.0 / D, stat[:, 6:7],
                        ALU.mult, ALU.subtract))
                    s_var2.inc(v.engine_nop())
                    v.wait_ge(s_rstd2.h, t + 1)
                    CE(v, vself, v.tensor_scalar(t1[:], hsb[:], stat[:, 2:3],
                                                 stat[:, 5:6], ALU.subtract,
                                                 ALU.mult))
                    CE(v, vself, v.tensor_tensor(t1[:], t1[:], gamF[:],
                                                 ALU.mult))
                    CE(v, vself, v.tensor_tensor(t1[:], t1[:], betF[:],
                                                 ALU.add))
                    v.wait_ge(ldg2[t % 2].h, 16 * (t // 2 + 1))
                    v.wait_ge(ldx2[t % 2].h, 16 * (t // 2 + 1))
                    if t >= 2:
                        v.wait_ge(sto[t % 2].h, 16 * (t // 2))
                    CE(v, vself, v.tensor_tensor(t1[:], t1[:], gb[t % 2][:],
                                                 ALU.mult))
                    CE(v, vself, v.tensor_tensor(xo[t % 2][:], xb[t % 2][:],
                                                 t1[:], ALU.add))
                    s_out.inc(v.engine_nop())



SCRATCH = {
    "xnT0": ([8, 128, TOK], BF16),
    "xnT1": ([8, 128, TOK], BF16),
    "xnb1": ([TOK, D], BF16),
    "cw0": ([NT, 128, 8], F32),
    "cw1": ([NT, 128, 8], F32),
    "g0": ([TOK, D], BF16),
    "g1": ([TOK, D], BF16),
    "ctxT0": ([8, 128, TOK], BF16),
    "ctxT1": ([8, 128, TOK], BF16),
    "xmid": ([TOK, D], F32),
    "gbias": ([16, 4095], BF16),
}

WEIGHT_SPECS = {
    "wq": ([D, D], F32), "wk": ([D, D], F32), "wv": ([D, D], F32),
    "wo": ([D, D], F32), "relb": ([32, 16], F32), "gateT": ([D, 8], F32),
    "w1T": ([8, D, 4096], F32), "w3T": ([8, D, 4096], F32),
    "w2T": ([8, 4096, D], F32),
    "wqkv": ([D, 3 * D], F32), "bqkv3": ([64, 48], F32),
    "bqkvr": ([48, 64], F32),
    "wo1": ([D, D], F32), "bo1": ([1, D], F32),
    "gam": ([1, D], F32), "bet": ([1, D], F32), "routT": ([D, 8], F32),
    "fw1": ([D, 8192], F32), "fw2": ([8192, D], F32),
}

CONST_NAMES = ["id128", "anti128", "ones1x128_bf", "onehotT",
               "cs_k", "sn_k", "cs_q", "sn_q", "rsign"]


def _mkprog(emit, ext_in, ext_out, consts_needed):
    nc = bass.Bass("TRN2", target_bir_lowering=False)
    aps = {}
    for nm in ext_in:
        sh, dt = WEIGHT_SPECS.get(nm) or SCRATCH.get(nm) or (([TOK, D], F32) if nm in ("x", "y") else None)
        aps[nm] = nc.dram_tensor(nm, sh, dt, kind="ExternalInput").ap()
    for nm in ext_out:
        sh, dt = SCRATCH.get(nm) or (([TOK, D], F32) if nm == "y" else None)
        aps[nm] = nc.dram_tensor(nm, sh, dt, kind="ExternalOutput").ap()
    consts = host_consts()
    with contextlib.ExitStack() as stack:
        block = stack.enter_context(nc.Block())
        c = Ctx(nc, stack, block)
        load_consts(c, consts, consts_needed)
        emit(c, aps)

        cse = c.cs["_sem"]

        def fin(sync):
            sync.wait_ge(cse.h, cse.n)
        c.block.sync(fin)
    return nc


def build_p1(L):
    ext_in = ["x", "gateT" if L == 0 else "routT"]
    ext_out = [f"xnT{L}", f"cw{L}"] + (["xnb1"] if L == 1 else [])
    aps_x = "x"

    def emit(c, aps):
        emit_p1(c, L, aps["x"],
                aps[f"xnT{L}"], aps[f"cw{L}"],
                aps.get("xnb1"), aps["gateT" if L == 0 else "routT"])
    return _mkprog(emit, ext_in, ext_out, ["id128", "fbank"])


def build_p2(L):
    if L == 0:
        ext_in = ["xnT0", "cw0", "w1T", "w3T", "w2T"]
        ext_out = ["g0"]

        def emit(c, aps):
            emit_p2(c, 0, aps["xnT0"], aps["cw0"], aps["g0"],
                    lambda e, g: aps["w1T"][e, :, g * 512:(g + 1) * 512],
                    lambda e, g: aps["w3T"][e, :, g * 512:(g + 1) * 512],
                    lambda e, g: aps["w2T"][e, g * 512:(g + 1) * 512, :],
                    NIC=32)
        return _mkprog(emit, ext_in, ext_out, ["fbank"])
    else:
        ext_in = ["xnT1", "cw1", "fw1", "fw2"]
        ext_out = ["g1"]

        def emit(c, aps):
            emit_p2(c, 1, aps["xnT1"], aps["cw1"], aps["g1"],
                    lambda e, g: aps["fw1"][:, e * 1024 + g * 512:
                                            e * 1024 + (g + 1) * 512],
                    None,
                    lambda e, g: aps["fw2"][e * 1024 + g * 512:
                                            e * 1024 + (g + 1) * 512, :],
                    NIC=8)
        return _mkprog(emit, ext_in, ext_out, ["fbank"])


def build_p0():
    def emit(c, aps):
        emit_gbias(c, aps["relb"], aps["gbias"])
    return _mkprog(emit, ["relb"], ["gbias"], ["onehotT"])


def build_p3(L):
    if L == 0:
        ext_in = ["xnT0", "wq", "wk", "wv", "gbias"]
        ext_out = ["ctxT0"]

        def emit(c, aps):
            emit_p3(c, 0, aps["xnT0"], aps["ctxT0"], wq_ap=aps["wq"],
                    wk_ap=aps["wk"], wv_ap=aps["wv"], gbias_ap=aps["gbias"])
        return _mkprog(emit, ext_in, ext_out,
                       ["id128", "anti128", "ones1x128_bf"])
    else:
        ext_in = ["xnT1", "wqkv", "bqkv3", "bqkvr"]
        ext_out = ["ctxT1"]

        def emit(c, aps):
            emit_p3(c, 1, aps["xnT1"], aps["ctxT1"], wqkv_ap=aps["wqkv"],
                    bqkv3_ap=aps["bqkv3"], bqkvr_ap=aps["bqkvr"])
        return _mkprog(emit, ext_in, ext_out,
                       ["id128", "ones1x128_bf", "cs_q", "sn_q",
                        "cs_k", "sn_k", "rsign"])


def build_p4(L):
    if L == 0:
        ext_in = ["ctxT0", "g0", "x", "wo"]
        ext_out = ["xmid"]

        def emit(c, aps):
            emit_p4(c, 0, aps["ctxT0"], aps["g0"], aps["x"], aps["xmid"],
                    aps["wo"])
        return _mkprog(emit, ext_in, ext_out, ["fbank", "ones1x128_bf"])
    else:
        ext_in = ["ctxT1", "g1", "xmid", "wo1", "bo1", "gam", "bet", "xnb1"]
        ext_out = ["y"]

        def emit(c, aps):
            emit_p4(c, 1, aps["ctxT1"], aps["g1"], aps["xmid"], aps["y"],
                    aps["wo1"], bo_ap=aps["bo1"], gam_ap=aps["gam"],
                    bet_ap=aps["bet"], xnb_ap=aps["xnb1"])
        return _mkprog(emit, ext_in, ext_out, ["fbank", "ones1x128_bf"])



_PROG_CACHE = {}


def _prog(key, builder):
    if key not in _PROG_CACHE:
        _PROG_CACHE[key] = builder()
    return _PROG_CACHE[key]


def kernel_unfused(**inputs):
    w = prep_weights(inputs)
    x_full = np.ascontiguousarray(
        np.asarray(inputs["L0_attn_hidden_states"], np.float32)
        .reshape(N_CORES, TOK, D))
    cores = list(range(N_CORES))

    def launch(nc, per_core):
        res = run_bass_kernel_spmd(nc, per_core, core_ids=cores)
        return res.results

    sh = [dict() for _ in cores]

    r = launch(_prog("p1_0", lambda: build_p1(0)),
               [{"x": x_full[i], "gateT": w["gateT"]} for i in cores])
    for i in cores:
        sh[i].update(r[i])
    r = launch(_prog("p0", build_p0), [{"relb": w["relb"]} for i in cores])
    for i in cores:
        sh[i].update(r[i])
    r = launch(_prog("p2_0", lambda: build_p2(0)),
               [{"xnT0": sh[i]["xnT0"], "cw0": sh[i]["cw0"], "w1T": w["w1T"],
                 "w3T": w["w3T"], "w2T": w["w2T"]} for i in cores])
    for i in cores:
        sh[i].update(r[i])
    r = launch(_prog("p3_0", lambda: build_p3(0)),
               [{"xnT0": sh[i]["xnT0"], "wq": w["wq"], "wk": w["wk"],
                 "wv": w["wv"], "gbias": sh[i]["gbias"]} for i in cores])
    for i in cores:
        sh[i].update(r[i])
    r = launch(_prog("p4_0", lambda: build_p4(0)),
               [{"ctxT0": sh[i]["ctxT0"], "g0": sh[i]["g0"], "x": x_full[i],
                 "wo": w["wo"]} for i in cores])
    for i in cores:
        sh[i].update(r[i])

    r = launch(_prog("p1_1", lambda: build_p1(1)),
               [{"x": sh[i]["xmid"], "routT": w["routT"]} for i in cores])
    for i in cores:
        sh[i].update(r[i])
    r = launch(_prog("p2_1", lambda: build_p2(1)),
               [{"xnT1": sh[i]["xnT1"], "cw1": sh[i]["cw1"], "fw1": w["fw1"],
                 "fw2": w["fw2"]} for i in cores])
    for i in cores:
        sh[i].update(r[i])
    r = launch(_prog("p3_1", lambda: build_p3(1)),
               [{"xnT1": sh[i]["xnT1"], "wqkv": w["wqkv"],
                 "bqkv3": w["bqkv3"], "bqkvr": w["bqkvr"]} for i in cores])
    for i in cores:
        sh[i].update(r[i])
    r = launch(_prog("p4_1", lambda: build_p4(1)),
               [{"ctxT1": sh[i]["ctxT1"], "g1": sh[i]["g1"],
                 "xmid": sh[i]["xmid"], "wo1": w["wo1"], "bo1": w["bo1"],
                 "gam": w["gam"], "bet": w["bet"], "xnb1": sh[i]["xnb1"]}
                for i in cores])

    y = np.concatenate([r[i]["y"].reshape(SEQ_PER_CORE, SEQ, D)
                        for i in cores], axis=0)
    return np.ascontiguousarray(y.astype(np.float32))



ALL_CONSTS = ["id128", "anti128", "ones1x128_bf", "fbank"]


def build_fused():
    nc = bass.Bass("TRN2", target_bir_lowering=False)
    aps = {}
    aps["x"] = nc.dram_tensor("x", [TOK, D], F32, kind="ExternalInput").ap()
    for nm, (sh, dt) in WEIGHT_SPECS.items():
        aps[nm] = nc.dram_tensor(nm, sh, dt, kind="ExternalInput").ap()
    aps["y"] = nc.dram_tensor("y", [TOK, D], F32, kind="ExternalOutput").ap()
    for nm, (sh, dt) in SCRATCH.items():
        aps[nm] = nc.dram_tensor(nm, sh, dt, kind="Internal").ap()

    consts = host_consts()
    with contextlib.ExitStack() as stack:
        block = stack.enter_context(nc.Block())
        c = Ctx(nc, stack, block)
        load_consts(c, consts, ALL_CONSTS)

        emit_p1(c, 0, aps["x"], aps["xnT0"], aps["cw0"], None, aps["gateT"])
        end_phase(c)
        emit_gbias(c, aps["relb"], aps["gbias"])
        end_phase(c)
        emit_p2(c, 0, aps["xnT0"], aps["cw0"], aps["g0"],
                lambda e, g: aps["w1T"][e, :, g * 512:(g + 1) * 512],
                lambda e, g: aps["w3T"][e, :, g * 512:(g + 1) * 512],
                lambda e, g: aps["w2T"][e, g * 512:(g + 1) * 512, :],
                NIC=32)
        end_phase(c)
        emit_p3(c, 0, aps["xnT0"], aps["ctxT0"], wq_ap=aps["wq"],
                wk_ap=aps["wk"], wv_ap=aps["wv"], gbias_ap=aps["gbias"])
        end_phase(c)
        emit_p4(c, 0, aps["ctxT0"], aps["g0"], aps["x"], aps["xmid"],
                aps["wo"])
        end_phase(c)
        emit_p1(c, 1, aps["xmid"], aps["xnT1"], aps["cw1"], aps["xnb1"],
                aps["routT"])
        end_phase(c)
        emit_p2(c, 1, aps["xnT1"], aps["cw1"], aps["g1"],
                lambda e, g: aps["fw1"][:, e * 1024 + g * 512:
                                        e * 1024 + (g + 1) * 512],
                None,
                lambda e, g: aps["fw2"][e * 1024 + g * 512:
                                        e * 1024 + (g + 1) * 512, :],
                NIC=8)
        end_phase(c)
        emit_p3(c, 1, aps["xnT1"], aps["ctxT1"], wqkv_ap=aps["wqkv"],
                bqkv3_ap=aps["bqkv3"], bqkvr_ap=aps["bqkvr"])
        end_phase(c)
        emit_p4(c, 1, aps["ctxT1"], aps["g1"], aps["xmid"], aps["y"],
                aps["wo1"], bo_ap=aps["bo1"], gam_ap=aps["gam"],
                bet_ap=aps["bet"], xnb_ap=aps["xnb1"])

        cse = c.cs["_sem"]

        def fin(sync):
            sync.wait_ge(cse.h, cse.n)
        c.block.sync(fin)
    return nc


def kernel(**inputs):
    w = prep_weights(inputs)
    x_full = np.ascontiguousarray(
        np.asarray(inputs["L0_attn_hidden_states"], np.float32)
        .reshape(N_CORES, TOK, D))
    nc = _prog("fused", build_fused)
    in_maps = []
    for i in range(N_CORES):
        m = {"x": x_full[i]}
        m.update(w)
        in_maps.append(m)
    res = run_bass_kernel_spmd(nc, in_maps, core_ids=list(range(N_CORES)))
    y = np.concatenate([res.results[i]["y"].reshape(SEQ_PER_CORE, SEQ, D)
                        for i in range(N_CORES)], axis=0)
    return np.ascontiguousarray(y.astype(np.float32))
